# version 06
# baseline (speedup 1.0000x reference)
import sys
if '/opt/trn_rl_repo' not in sys.path:
    sys.path.insert(0, '/opt/trn_rl_repo')
import numpy as np
import ml_dtypes
from contextlib import ExitStack

import concourse.bass as bass
import concourse.mybir as mybir
from concourse.bass_utils import run_bass_kernel_spmd
from concourse.expressions_rust import RuntimeValue

F32 = mybir.dt.float32
BF16 = mybir.dt.bfloat16
I32 = mybir.dt.int32
AF = mybir.ActivationFunctionType
ALU = mybir.AluOpType

P = 128
B, T, D = 4, 2048, 2048
TOK = 1024
NT = 512
DC = D // P
H0, H0C = 5632, 44
H1, H1C = 8192, 64
XNSTR = TOK + 1
LN_EPS = 1e-5
QEPS = 1e-8

ENGINES = ("sync", "scalar", "vector", "tensor", "gpsimd")


def _f32(a):
    return np.ascontiguousarray(a, dtype=np.float32)


def pack_panels(WT):
    din, dout = WT.shape
    KC, MC = din // P, dout // P
    t = _f32(WT).reshape(KC, P, MC, P).transpose(2, 1, 0, 3)
    return _f32(t.reshape(MC, P, KC * P))


def pack_x_shard(xbt):
    tc = xbt.shape[0]
    t = _f32(xbt).T.reshape(DC, P, tc).transpose(1, 0, 2)
    return _f32(t.reshape(P, DC * tc))


def unpack_x_shard(arr, tc=TOK):
    t = np.asarray(arr).reshape(P, DC, tc).transpose(1, 0, 2)
    return t.reshape(D, tc).T


def pack_vec(v):
    return _f32(np.asarray(v).reshape(-1, P).T)


def hadamard128_bf16():
    h = np.array([[1.0]], dtype=np.float32)
    while h.shape[0] < P:
        h = np.block([[h, h], [h, -h]])
    return h.astype(ml_dtypes.bfloat16)



class KB:
    def __init__(self, nc, blk, sems):
        self.nc, self.blk, self.sems = nc, blk, sems
        self.cnt = {k: 0 for k in sems}
        self.xw_role = {"w": 0, "r": 0}
        self.waited = {e: {} for e in ENGINES}
        self.bank_free = [None] * 8
        self.pb = None
        self.stream_guard = (None, None)

    def body(self, eng, fn):
        getattr(self.blk, eng)(fn)

    def bump(self, s, n):
        self.cnt[s] += n
        return self.cnt[s]

    def mark(self, s):
        return (s, self.cnt[s])

    def W(self, E, eng, mk):
        if mk is None:
            return
        s, v = mk
        if v <= 0 or self.waited[eng].get(s, 0) >= v:
            return
        self.waited[eng][s] = v
        E.wait_ge(self.sems[s], v)

    def dma(self, E, eng, q, out, in_, waits=()):
        for m in waits:
            self.W(E, eng, m)
        v = self.bump(q, 16)
        E.dma_start(out=out, in_=in_).then_inc(self.sems[q], 16)
        return (q, v)

    def fin(self, inst, s):
        v = self.bump(s, 1)
        inst.then_inc(self.sems[s], 1)
        return (s, v)

    def bank(self, E, eng, i):
        self.W(E, eng, self.bank_free[i])
        return self.pb[i]


class Rot:

    def __init__(self, kb, base):
        self.kb = kb
        self.names = [f"{base}_{i}" for i in range(3)]
        self.n = 0

    def issue(self, E, out, in_):
        nm = self.names[self.n % 3]
        E.dma_start(out=out, in_=in_).then_inc(self.kb.sems[nm], 16)
        self.kb.bump(nm, 16)
        self.n += 1
        return self.n - 1

    def mark(self, i):
        return (self.names[i % 3], 16 * (i // 3 + 1))


def build_program(with_l0=True, with_l1=True, debug=False):
    nc = bass.Bass("TRN2", target_bir_lowering=False)

    d = {}
    def EIN(name, shape, dt=F32):
        d[name] = nc.dram_tensor(name, shape, dt, kind="ExternalInput").ap()

    EIN("ctl", [1, 12], I32)
    if with_l0:
        EIN("xin", [P, DC * TOK])
        EIN("w0p", [DC, P, D])
        EIN("w1p", [H0C, P, D])
        EIN("w2p", [H0C, P, D])
        EIN("w3p", [DC, P, H0])
        EIN("b0t", [P, DC])
    if with_l1:
        if not with_l0:
            EIN("x1in", [P, DC * TOK])
        for nm in ("wkp", "wvp", "wrp", "wop"):
            EIN(nm, [DC, P, D])
        for nm in ("bkt", "bvt", "brt", "bot", "tmkt", "tmvt", "tmrt", "ut", "wt",
                   "sugt", "wsdpt"):
            EIN(nm, [P, DC])
        for nm in ("sdpt", "sut", "sgt"):
            EIN(nm, [P, H1C])
        EIN("wup", [H1C, P, D])
        EIN("wgp", [H1C, P, D])
        EIN("wdp", [DC, P, H1])
        EIN("h128", [P, P], BF16)

    dbg = {}
    if debug and with_l0:
        for nm, cols, dt_ in (("dbg_xn", DC * XNSTR, BF16), ("dbg_lam", DC * TOK, F32),
                              ("dbg_s0", DC * TOK, BF16), ("dbg_h0", H0C * TOK, BF16),
                              ("dbg_tq", 16, F32), ("dbg_sin", 2 * DC, F32), ("dbg_gj", 40, F32), ("dbg_rows", 16, F32)):
            dbg[nm] = nc.dram_tensor(nm, [P, cols], dt_, kind="ExternalOutput").ap()
    if debug and with_l1:
        for nm, cols, dt_ in (("dbg1_xn", DC * XNSTR, BF16), ("dbg1_k", DC * TOK, BF16),
                              ("dbg1_v", DC * TOK, BF16), ("dbg1_r", DC * TOK, BF16),
                              ("dbg1_ab", 4 * DC, F32), ("dbg1_wkvr", DC * TOK, BF16),
                              ("dbg1_z", DC * TOK, BF16), ("dbg1_z2", DC * TOK, BF16), ("dbg1_sb", DC * NT, BF16), ("dbg1_h", H1C * TOK, BF16),
                              ("dbg1_hh", H1C * TOK, BF16), ("dbg1_s1", DC * TOK, BF16)):
            dbg[nm] = nc.dram_tensor(nm, [P, cols], dt_, kind="ExternalOutput").ap()
    if with_l0 and not with_l1:
        x1buf = nc.dram_tensor("x1out", [P, DC * TOK], F32, kind="ExternalOutput").ap()
    elif with_l0:
        x1buf = nc.dram_tensor("x1mid", [P, DC * TOK], F32).ap()
    else:
        x1buf = d["x1in"]
    yout = s1sp = None
    if with_l1:
        yout = nc.dram_tensor("yout", [P, DC * TOK], F32, kind="ExternalOutput").ap()
        s1sp = nc.dram_tensor("s1spill", [P, DC * TOK], BF16).ap()

    shr = {}
    def SHR(name, shape, dt=F32):
        shr[name] = nc.dram_tensor(name, shape, dt, addr_space="Shared").ap()
    if with_l0:
        SHR("pay_g0", [P, 4]); SHR("pay_g1", [P, 4])
        SHR("flag_g0", [1, 16], I32); SHR("flag_g1", [1, 16], I32)
        SHR("pay_s", [P, DC]); SHR("flag_s", [1, 16], I32)
    if with_l1:
        SHR("pay_xn", [P, DC], BF16); SHR("flag_xn", [1, 16], I32)
        SHR("pay_ab", [P, 2 * DC]); SHR("flag_ab", [1, 16], I32)

    SEMNAMES = ["qS", "qA", "qG", "eT", "eS", "eV", "eP", "xd",
                "xw", "xp_g", "xp_g2", "xp_s", "xp_xn", "xp_ab",
                "qS_0", "qS_1", "qS_2", "qG_0", "qG_1", "qG_2",
                "qX_0", "qX_1", "qX_2"]

    with ExitStack() as ST:
        E_ = ST.enter_context
        ARENA_COLS = 49168
        arena_t = E_(nc.sbuf_tensor("arena", [P, ARENA_COLS], F32))
        arena_f = arena_t.ap()
        arena_b = arena_f.bitcast(BF16)

        def AFP(off, cols):
            assert off % 16 == 0 and off + cols * 4 <= ARENA_COLS * 4, (off, cols)
            return arena_f[:, off // 4: off // 4 + cols]

        def ABF(off, cols):
            assert off % 16 == 0 and off + cols * 2 <= ARENA_COLS * 4, (off, cols)
            return arena_b[:, off // 2: off // 2 + cols]

        ctl_sb = E_(nc.sbuf_tensor("ctl_sb", [1, 12], I32))
        fsb = E_(nc.sbuf_tensor("fsb", [1, 16], I32))
        flgsrc = E_(nc.sbuf_tensor("flgsrc", [1, 16], I32))
        pcnt = E_(nc.sbuf_tensor("pcnt", [1, 8], I32))
        ones_b = E_(nc.sbuf_tensor("ones_b", [P, 1], BF16))
        onesf_col = E_(nc.sbuf_tensor("onesf_col", [P, 1], F32))
        onesf_row = E_(nc.sbuf_tensor("onesf_row", [1, P], F32))
        smalls = E_(nc.sbuf_tensor("smalls", [P, 512], F32))
        rows = E_(nc.sbuf_tensor("rows", [1, 4 * NT], F32))
        grow_t = E_(nc.sbuf_tensor("grow_t", [1, 4], F32))
        trow_t = E_(nc.sbuf_tensor("trow_t", [1, 4], F32))
        if with_l1:
            sdp_sb = E_(nc.sbuf_tensor("sdp_sb", [P, H1C], F32))
            su2_sb = E_(nc.sbuf_tensor("su2_sb", [P, H1C], F32))
            sg_sb = E_(nc.sbuf_tensor("sg_sb", [P, H1C], F32))
            h128_sb = E_(nc.sbuf_tensor("h128_sb", [P, P], BF16))
        psb = [E_(nc.psum_tensor(f"pb{i}", [P, NT], F32)) for i in range(8)]
        sems = {n: E_(nc.semaphore(n)) for n in SEMNAMES}
        blk = E_(nc.Block())

        kb = KB(nc, blk, sems)
        kb.pb = [t.ap() for t in psb]
        kb.rS = Rot(kb, "qS")
        kb.rG = Rot(kb, "qG")
        kb.rX = Rot(kb, "qX")
        body = kb.body

        SM = {"_off": 0}
        def sm(n):
            off = SM["_off"]; SM["_off"] = off + n
            assert SM["_off"] <= 512
            return smalls.ap()[:, off:off + n]
        b0sb = sm(DC)
        accg, accg2 = sm(4), sm(4)
        Tq, nTq = sm(4), sm(4)
        sfin, s_in = sm(DC), sm(DC)
        redtmp = sm(4)
        gpart = sm(4)
        gjd = sm(40)
        if with_l1:
            bk_sb, bv_sb, br_sb, bo_sb = sm(DC), sm(DC), sm(DC), sm(DC)
            tmk_sb, tmv_sb, tmr_sb = sm(DC), sm(DC), sm(DC)
            u_sb, w_sb, nw_sb, uw_sb = sm(DC), sm(DC), sm(DC), sm(DC)
            sug_sb, wsdp_sb = sm(DC), sm(DC)
            afin, bfin, a_in, b_in = sm(DC), sm(DC), sm(DC), sm(DC)

        ssum = rows.ap()[0:1, 0:NT]
        sqsum = rows.ap()[0:1, NT:2 * NT]
        mrow = rows.ap()[0:1, 2 * NT:3 * NT]
        rrow = rows.ap()[0:1, 3 * NT:4 * NT]

        def XN(c, lo, n):
            start = c * XNSTR + 1 + lo
            return arena_b[:, start:start + n]

        _rid = [0]

        def rid(base):
            _rid[0] += 1
            return f"{base}{_rid[0]}"

        sq_strips = [ABF(188480, NT), ABF(188480 + 1024, NT)]
        xb_strips = [ABF(188480 + 2048, NT), ABF(188480 + 3072, NT)]
        lntmp = AFP(192576, NT)

        def _poll(E, flag_dram, fexp_reg, xq, slot):
            sem = sems[xq]
            with E.register(rid("rc")) as rcnt, E.register(rid("fc")) as fcur:
                E.reg_mov(rcnt, 0)
                E.dma_start(out=fsb[:], in_=flag_dram).then_inc(sem, 16)
                E.reg_add(rcnt, rcnt, 16)
                E.wait_ge(sem, rcnt)
                E.load(fcur, fsb.ap()[0:1, 0:1])

                def cond():
                    return RuntimeValue(fcur) - RuntimeValue(fexp_reg)
                with E.While(cond):
                    E.dma_start(out=fsb[:], in_=flag_dram).then_inc(sem, 16)
                    E.reg_add(rcnt, rcnt, 16)
                    E.wait_ge(sem, rcnt)
                    E.load(fcur, fsb.ap()[0:1, 0:1])
                E.reg_save(pcnt.ap()[0:1, slot:slot + 1], rcnt)

        def _read_payloads(E, xq, slot, pairs):
            for dst, src in pairs:
                E.dma_start(out=dst, in_=src).then_inc(sems[xq], 16)
            with E.register(rid("rw")) as r:
                E.load(r, pcnt.ap()[0:1, slot:slot + 1])
                E.reg_add(r, r, 16 * len(pairs))
                E.wait_ge(sems[xq], r)

        def _write_payloads(E, role, pairs, flag_dram, nonce_col):
            base = kb.xw_role[role]
            for dst, src in pairs:
                E.dma_start(out=dst, in_=src).then_inc(sems["xw"], 16)
            kb.xw_role[role] += 16 * len(pairs)
            E.wait_ge(sems["xw"], kb.xw_role[role])
            with E.register(rid("rn")) as rn:
                E.load(rn, ctl_sb.ap()[0:1, nonce_col:nonce_col + 1])
                E.store(flgsrc.ap()[0:1, 0:1], rn)
            E.dma_start(out=flag_dram, in_=flgsrc[:]).then_inc(sems["xw"], 16)
            kb.xw_role[role] += 16

        XSLOT = {"xp_g": 0, "xp_g2": 4, "xp_s": 1, "xp_xn": 2, "xp_ab": 3}

        def emit_exchange(site, nonce_col, pay_w, pay_r, ready_mk, extra_mks=(),
                          w_role="w", r_is="is_r"):
            xq = f"xp_{site}"
            fname = {"g": "flag_g0", "g2": "flag_g1"}.get(site, f"flag_{site}")
            flag = shr[fname]

            def f_gp(E):
                kb.W(E, "gpsimd", ready_mk)
                for m in extra_mks:
                    kb.W(E, "gpsimd", m)
                with E.register(rid("isw")) as is_w, E.register(rid("isr")) as is_r, \
                     E.register(rid("fxp")) as fexp:
                    wcol = 0 if w_role == "w" else 1
                    E.load(is_w, ctl_sb.ap()[0:1, wcol:wcol + 1])
                    E.load(is_r, ctl_sb.ap()[0:1, (1 - wcol):(2 - wcol)])
                    E.load(fexp, ctl_sb.ap()[0:1, nonce_col:nonce_col + 1])
                    with E.If(is_w):
                        _write_payloads(E, w_role, pay_w, flag, nonce_col)
                    E.end_ifs()
                    with E.If(is_r):
                        _poll(E, flag, fexp, xq, XSLOT[xq])
                        _read_payloads(E, xq, XSLOT[xq], pay_r)
                    E.end_ifs()
                E.sem_inc(sems["xd"], 1)
            body("gpsimd", f_gp)
            return ("xd", kb.bump("xd", 1))

        STR_OFF = 163904

        def stream_panels(jobs, kchunks, kgroups, rhs_fn, evac_fn, quant_col=None,
                          cast_pre=(), banks=(0, 1, 2, 3), qtmp_off=188480,
                          str_off=None, str_cap=188480, dma_pre=(), cast_eng="scalar",
                          pool_cols=None):
            if str_off is None:
                str_off = STR_OFF
            n_j = len(jobs)
            kcg = kchunks // kgroups
            fcols = kcg * P
            fsz16 = ((fcols * 4 + 15) // 16) * 16
            bsz16 = ((fcols * 2 + 15) // 16) * 16
            ftiles = [AFP(str_off, fcols), AFP(str_off + fsz16, fcols)]
            boff = str_off + 2 * fsz16
            btiles = [ABF(boff, fcols), ABF(boff + bsz16, fcols)]
            assert boff + 2 * bsz16 <= str_cap
            qtmp = ABF(qtmp_off, fcols)
            n_it = n_j * kgroups
            qc = None
            if quant_col is not None and quant_col != "pool_cmp":
                qc = quant_col if callable(quant_col) else (lambda it: quant_col)

            prev_cast, prev_mm = kb.stream_guard
            cast_mks = [None] * n_it
            mm_mks = {}

            def bank_of(j, h):
                return banks[(2 * j + h) % len(banks)]

            for it in range(n_it):
                j, g = divmod(it, kgroups)

                def f_dma(E, it=it, j=j, g=g):
                    if it == 0 and prev_cast is not None:
                        kb.W(E, "sync", prev_cast)
                    if it == 0:
                        for m_ in dma_pre:
                            kb.W(E, "sync", m_)
                    if it >= 2:
                        kb.W(E, "sync", cast_mks[it - 2])
                    kb.rS.issue(E, ftiles[it % 2][:, 0:fcols],
                                jobs[j][:, g * fcols:(g + 1) * fcols])
                body("sync", f_dma)
                dma_mk = kb.rS.mark(kb.rS.n - 1)

                if quant_col == "pool_cmp":
                    pool_col = pool_cols if callable(pool_cols) else (lambda it: 0)
                    getmp = ABF(qtmp_off, fcols)
                    letmp = ABF(qtmp_off + ((fcols * 2 + 15) // 16) * 16, fcols)

                    def f_cmp(E, it=it, dma_mk=dma_mk):
                        for m in cast_pre:
                            kb.W(E, "gpsimd", m)
                        if it == 0 and prev_mm is not None:
                            kb.W(E, "gpsimd", prev_mm)
                        kb.W(E, "gpsimd", dma_mk)
                        if it >= 1:
                            kb.W(E, "gpsimd", cast_mks[it - 1])
                        col = pool_col(it)
                        E.tensor_scalar(out=getmp, in0=ftiles[it % 2][:, 0:fcols],
                                        scalar1=Tq[:, col:col + 1], scalar2=None,
                                        op0=ALU.is_ge)
                        kb.fin(E.tensor_scalar(out=letmp, in0=ftiles[it % 2][:, 0:fcols],
                                               scalar1=nTq[:, col:col + 1], scalar2=None,
                                               op0=ALU.is_le), "eP")
                    body("gpsimd", f_cmp)
                    cmp_mk = kb.mark("eP")

                    def f_sub(E, it=it, cmp_mk=cmp_mk):
                        kb.W(E, "vector", cmp_mk)
                        if it >= 2:
                            kb.W(E, "vector", mm_mks[(it - 2, 1)])
                        kb.fin(E.tensor_tensor(out=btiles[it % 2][:, 0:fcols],
                                               in0=getmp, in1=letmp,
                                               op=ALU.subtract), "eV")
                    body("vector", f_sub)
                    cast_mks[it] = kb.mark("eV")
                    ftfree_mks = None
                elif qc is not None:
                    def f_sgn(E, it=it, dma_mk=dma_mk):
                        for m in cast_pre:
                            kb.W(E, "scalar", m)
                        if it == 0 and prev_mm is not None:
                            kb.W(E, "scalar", prev_mm)
                        kb.W(E, "scalar", dma_mk)
                        if it >= 2:
                            kb.W(E, "scalar", mm_mks[(it - 2, 1)])
                        if it >= 1:
                            kb.W(E, "scalar", cast_mks[it - 1])
                        col = qc(it)
                        E.sign(out=btiles[it % 2][:, 0:fcols],
                               in_=ftiles[it % 2][:, 0:fcols], bias=nTq[:, col:col + 1])
                        kb.fin(E.sign(out=qtmp, in_=ftiles[it % 2][:, 0:fcols],
                                      bias=Tq[:, col:col + 1]), "eS")
                    body("scalar", f_sgn)
                    sgn_mk = kb.mark("eS")

                    def f_add(E, it=it, sgn_mk=sgn_mk):
                        kb.W(E, "vector", sgn_mk)
                        kb.fin(E.tensor_tensor(out=btiles[it % 2][:, 0:fcols],
                                               in0=btiles[it % 2][:, 0:fcols], in1=qtmp,
                                               op=ALU.add), "eV")
                    body("vector", f_add)
                    cast_mks[it] = kb.mark("eV")
                elif cast_eng == "gpsimd":
                    def f_cst(E, it=it, dma_mk=dma_mk):
                        for m in cast_pre:
                            kb.W(E, "gpsimd", m)
                        if it == 0 and prev_mm is not None:
                            kb.W(E, "gpsimd", prev_mm)
                        kb.W(E, "gpsimd", dma_mk)
                        if it >= 2:
                            kb.W(E, "gpsimd", mm_mks[(it - 2, 1)])
                        kb.fin(E.tensor_copy(out=btiles[it % 2][:, 0:fcols],
                                             in_=ftiles[it % 2][:, 0:fcols]), "eP")
                    body("gpsimd", f_cst)
                    cast_mks[it] = kb.mark("eP")
                else:
                    def f_cst(E, it=it, dma_mk=dma_mk):
                        for m in cast_pre:
                            kb.W(E, "scalar", m)
                        if it == 0 and prev_mm is not None:
                            kb.W(E, "scalar", prev_mm)
                        kb.W(E, "scalar", dma_mk)
                        if it >= 2:
                            kb.W(E, "scalar", mm_mks[(it - 2, 1)])
                        kb.fin(E.copy(out=btiles[it % 2][:, 0:fcols],
                                      in_=ftiles[it % 2][:, 0:fcols]), "eS")
                    body("scalar", f_cst)
                    cast_mks[it] = kb.mark("eS")

                def f_mm(E, it=it, j=j, g=g):
                    kb.W(E, "tensor", cast_mks[it])
                    for h in range(2):
                        bank = kb.bank(E, "tensor", bank_of(j, h))
                        inst = None
                        for k in range(kcg):
                            inst = E.matmul(bank, btiles[it % 2][:, k * P:(k + 1) * P],
                                            rhs_fn(g * kcg + k, h),
                                            start=(g == 0 and k == 0),
                                            stop=(g == kgroups - 1 and k == kcg - 1))
                        kb.fin(inst, "eT")
                body("tensor", f_mm)
                for h in range(2):
                    mm_mks[(it, h)] = ("eT", kb.cnt["eT"] - 1 + h)

                if g == kgroups - 1:
                    for h in range(2):
                        rd = evac_fn(j, h, bank_of(j, h), mm_mks[(it, h)])
                        kb.bank_free[bank_of(j, h)] = rd

            kb.stream_guard = (cast_mks[-1], mm_mks[(n_it - 1, 1)])

        if with_l0:
            def f_c1(E):
                kb.dma(E, "scalar", "qA", ctl_sb[:], d["ctl"])
                kb.dma(E, "scalar", "qA", b0sb, d["b0t"])
            body("scalar", f_c1)
            ctl_mk = kb.mark("qA")

            def f_c2(E):
                E.memset(ones_b[:], 1.0)
                E.memset(onesf_col[:], 1.0)
                E.memset(onesf_row[:], 1.0)
                E.memset(s_in, 0.0)
                E.memset(accg, 0.0)
                E.memset(accg2, 0.0)
                kb.fin(E.memset(flgsrc[:], 0), "eV")
            body("vector", f_c2)
            consts_mk = kb.mark("eV")

            XOFF = 32832
            Xf = AFP(XOFF, DC * TOK)

            def xch(c, h=None):
                if h is None:
                    return Xf[:, c * TOK:(c + 1) * TOK]
                return Xf[:, c * TOK + h * NT:c * TOK + (h + 1) * NT]

            def f_xl(E):
                kb.dma(E, "scalar", "qA", Xf, d["xin"])
            body("scalar", f_xl)
            x_mk = kb.mark("qA")

            xn_done = {}
            for h in range(2):
                eS0 = kb.cnt["eS"]; eV0 = kb.cnt["eV"]; eT0 = kb.cnt["eT"]

                def f_sq(E, h=h, eT0=eT0):
                    kb.W(E, "scalar", x_mk)
                    for c in range(DC):
                        if c >= 2:
                            E.wait_ge(sems["eT"], eT0 + c - 1)
                        E.square(out=sq_strips[c % 2], in_=xch(c, h)).then_inc(sems["eS"], 1)
                body("scalar", f_sq)
                kb.bump("eS", DC)

                def f_cp(E, h=h, eT0=eT0):
                    kb.W(E, "vector", x_mk)
                    kb.W(E, "vector", consts_mk)
                    for c in range(DC):
                        if c >= 2:
                            E.wait_ge(sems["eT"], eT0 + c - 1)
                        E.tensor_copy(out=xb_strips[c % 2], in_=xch(c, h)).then_inc(sems["eV"], 1)
                body("vector", f_cp)
                kb.bump("eV", DC)

                bA, bB = (0, 1) if h == 0 else (2, 3)

                def f_st(E, h=h, bA=bA, bB=bB, eS0=eS0, eV0=eV0):
                    bankA = kb.bank(E, "tensor", bA)
                    bankB = kb.bank(E, "tensor", bB)
                    for c in range(DC):
                        E.wait_ge(sems["eS"], eS0 + c + 1)
                        E.wait_ge(sems["eV"], eV0 + c + 1)
                        E.matmul(bankA[0:1, :], ones_b[:], xb_strips[c % 2],
                                 start=(c == 0), stop=(c == DC - 1))
                        E.matmul(bankB[0:1, :], ones_b[:], sq_strips[c % 2],
                                 start=(c == 0), stop=(c == DC - 1)).then_inc(sems["eT"], 1)
                        kb.bump("eT", 1)
                body("tensor", f_st)
                smm_mk = kb.mark("eT")

                def f_sf(E, bA=bA, bB=bB, smm_mk=smm_mk):
                    kb.W(E, "vector", smm_mk)
                    E.tensor_scalar(out=mrow, in0=kb.pb[bA][0:1, :], scalar1=1.0 / D,
                                    scalar2=None, op0=ALU.mult)
                    E.tensor_scalar(out=ssum, in0=kb.pb[bB][0:1, :], scalar1=1.0 / D,
                                    scalar2=None, op0=ALU.mult)
                    E.tensor_tensor(out=sqsum, in0=mrow, in1=mrow, op=ALU.mult)
                    E.tensor_tensor(out=ssum, in0=ssum, in1=sqsum, op=ALU.subtract)
                    kb.fin(E.tensor_scalar(out=ssum, in0=ssum, scalar1=LN_EPS,
                                           scalar2=None, op0=ALU.add), "eV")
                body("vector", f_sf)
                var_mk = kb.mark("eV")
                kb.bank_free[bA] = var_mk
                kb.bank_free[bB] = var_mk

                def f_sqr(E, var_mk=var_mk):
                    kb.W(E, "scalar", var_mk)
                    kb.fin(E.sqrt(out=sqsum, in_=ssum), "eS")
                body("scalar", f_sqr)
                sd_mk = kb.mark("eS")

                def f_rc(E, sd_mk=sd_mk):
                    kb.W(E, "vector", sd_mk)
                    kb.fin(E.reciprocal(out=rrow, in_=sqsum), "eV")
                body("vector", f_rc)
                rstd_mk = kb.mark("eV")

                bM, bR = (4, 5) if h == 0 else (6, 7)

                def f_bc(E, bM=bM, bR=bR, rstd_mk=rstd_mk):
                    kb.W(E, "tensor", rstd_mk)
                    bankM = kb.bank(E, "tensor", bM)
                    bankR = kb.bank(E, "tensor", bR)
                    E.matmul(bankM, onesf_row[:], mrow, start=True, stop=True)
                    kb.fin(E.matmul(bankR, onesf_row[:], rrow, start=True, stop=True), "eT")
                body("tensor", f_bc)
                bc_mk = kb.mark("eT")

                def f_xn(E, h=h, bM=bM, bR=bR, bc_mk=bc_mk):
                    kb.W(E, "vector", bc_mk)
                    for c in range(DC):
                        E.tensor_tensor(out=lntmp, in0=xch(c, h), in1=kb.pb[bM],
                                        op=ALU.subtract)
                        E.tensor_tensor(out=XN(c, h * NT, NT), in0=lntmp, in1=kb.pb[bR],
                                        op=ALU.mult)
                    kb.fin(E.engine_nop(), "eV")
                body("vector", f_xn)
                xn_done[h] = kb.mark("eV")
                kb.bank_free[bM] = xn_done[h]
                kb.bank_free[bR] = xn_done[h]

            xn_mk = xn_done[1]

            gtiles = [AFP(98368, 4096), AFP(114752, 4096),
                      AFP(131136, 4096), AFP(147520, 4096)]
            w1v = d["w1p"].rearrange("m p c -> p m c")
            w2v = d["w2p"].rearrange("m p c -> p m c")
            w3v = d["w3p"].rearrange("m p c -> p m c")
            gjobs = []
            for i in range(11):
                gjobs.append((0, w1v[:, 2 * i:2 * i + 2, :], w1v[:, 22 + 2 * i:24 + 2 * i, :], 4096))
            for i in range(11):
                gjobs.append((1, w2v[:, 2 * i:2 * i + 2, :], w2v[:, 22 + 2 * i:24 + 2 * i, :], 4096))
            for i in range(16):
                m0, k0 = i // 2, (i % 2) * 2816
                gjobs.append((2, w3v[:, m0:m0 + 1, k0:k0 + 2816],
                              w3v[:, 8 + m0:9 + m0, k0:k0 + 2816], 2816))
            nred = len(gjobs)
            eV0g = kb.cnt["eV"]
            grot0 = kb.rG.n

            def f_gd(E):
                kb.W(E, "gpsimd", ctl_mk)
                with E.register(rid("gisw")) as is_w:
                    E.load(is_w, ctl_sb.ap()[0:1, 0:1])
                    with E.If(is_w):
                        for i, (j, va, vb, n) in enumerate(gjobs):
                            if i >= 4:
                                E.wait_ge(sems["eV"], eV0g + i - 3)
                            nm = kb.rG.names[(grot0 + i) % 3]
                            E.dma_start(out=gtiles[i % 4][:, 0:n], in_=va).then_inc(sems[nm], 16)
                    E.end_ifs()
                    with E.If_eq(is_w, 0):
                        for i, (j, va, vb, n) in enumerate(gjobs):
                            if i >= 4:
                                E.wait_ge(sems["eV"], eV0g + i - 3)
                            nm = kb.rG.names[(grot0 + i) % 3]
                            E.dma_start(out=gtiles[i % 4][:, 0:n], in_=vb).then_inc(sems[nm], 16)
                    E.end_ifs()
            body("gpsimd", f_gd)
            for i in range(nred):
                kb.bump(kb.rG.names[(grot0 + i) % 3], 16)
                kb.rG.n += 1
            gdma_mk = kb.rG.mark(kb.rG.n - 1)

            def f_gr(E):
                kb.W(E, "vector", consts_mk)
                def add_of(i):
                    j = gjobs[i][0]
                    E.tensor_tensor(out=accg[:, j:j + 1], in0=accg[:, j:j + 1],
                                    in1=redtmp[:, (i % 2):(i % 2) + 1],
                                    op=ALU.add).then_inc(sems["eV"], 1)
                for i, (j, va, vb, n) in enumerate(gjobs):
                    gi = grot0 + i
                    E.wait_ge(sems[kb.rG.names[gi % 3]], 16 * (gi // 3 + 1))
                    E.tensor_reduce(out=redtmp[:, (i % 2):(i % 2) + 1],
                                    in_=gtiles[i % 4][:, 0:n],
                                    axis=mybir.AxisListType.X, op=ALU.add,
                                    apply_absolute_value=True)
                    if i >= 1:
                        add_of(i - 1)
                E.tensor_copy(out=gtiles[0][:, 0:256], in_=gtiles[0][:, 256:512])
                add_of(nred - 1)
            body("vector", f_gr)
            kb.bump("eV", nred)

            def f_grc(E):
                kb.fin(E.engine_nop(), "eV")
            body("vector", f_grc)
            gred_mk = kb.mark("eV")

            xd_g = emit_exchange("g", 2, [(shr["pay_g0"], accg)],
                                 [(accg2, shr["pay_g0"])], ready_mk=gred_mk,
                                 extra_mks=(consts_mk,), w_role="w")
            xd_g2 = emit_exchange("g2", 2, [(shr["pay_g1"], accg)],
                                  [(accg2, shr["pay_g1"])], ready_mk=gred_mk,
                                  w_role="r")



            LOFF = 32832
            LAM = AFP(LOFF, DC * TOK)

            def lam_ch(c, h=None):
                if h is None:
                    return LAM[:, c * TOK:(c + 1) * TOK]
                return LAM[:, c * TOK + h * NT:c * TOK + (h + 1) * NT]

            lam_evac_mks = {}

            def lam_evac(m, h, bidx, mm_mk):
                def f(E, m=m, h=h, bidx=bidx, mm_mk=mm_mk):
                    kb.W(E, "scalar", mm_mk)
                    kb.fin(E.activation(out=lam_ch(m, h), in_=kb.pb[bidx],
                                        func=AF.Sigmoid, bias=b0sb[:, m:m + 1]), "eS")
                body("scalar", f)
                lam_evac_mks[(m, h)] = kb.mark("eS")
                return lam_evac_mks[(m, h)]

            stream_panels([d["w0p"][m] for m in range(DC)], DC, 1,
                          rhs_fn=lambda k, h: XN(k, h * NT, NT),
                          evac_fn=lam_evac, cast_pre=(xn_mk,))

            def f_gf(E):
                kb.W(E, "vector", xd_g)
                kb.W(E, "vector", xd_g2)
                E.tensor_copy(out=gpart, in_=accg)
                kb.fin(E.tensor_tensor(out=accg, in0=accg, in1=accg2, op=ALU.add), "eV")
            body("vector", f_gf)
            gtot_mk = kb.mark("eV")

            def f_gt(E):
                kb.W(E, "tensor", gtot_mk)
                bank = kb.bank(E, "tensor", 6)
                kb.fin(E.matmul(bank[0:1, 0:4], onesf_col[:], accg, start=True,
                                stop=True), "eT")
            body("tensor", f_gt)
            gmm_mk = kb.mark("eT")

            def f_gs(E, gmm_mk=gmm_mk):
                kb.W(E, "vector", gmm_mk)
                E.tensor_scalar(out=grow_t[:], in0=kb.pb[6][0:1, 0:4],
                                scalar1=1.0 / (H0 * D), scalar2=None, op0=ALU.mult)
                kb.fin(E.tensor_scalar(out=trow_t[:], in0=kb.pb[6][0:1, 0:4],
                                       scalar1=0.5 / (H0 * D),
                                       scalar2=0.5 * QEPS, op0=ALU.mult, op1=ALU.add), "eV")
            body("vector", f_gs)
            kb.bank_free[6] = kb.mark("eV")
            trow_mk = kb.mark("eV")

            def f_gt2(E):
                kb.W(E, "tensor", trow_mk)
                bank = kb.bank(E, "tensor", 7)
                kb.fin(E.matmul(bank[:, 0:4], onesf_row[:], trow_t[:], start=True,
                                stop=True), "eT")
            body("tensor", f_gt2)
            tb_mk = kb.mark("eT")

            def f_gv(E, tb_mk=tb_mk):
                kb.W(E, "vector", tb_mk)
                E.tensor_copy(out=Tq, in_=kb.pb[7][:, 0:4])
                kb.fin(E.tensor_scalar(out=nTq, in0=kb.pb[7][:, 0:4], scalar1=-1.0,
                                       scalar2=None, op0=ALU.mult), "eV")
            body("vector", f_gv)
            kb.bank_free[7] = kb.mark("eV")
            gamma_mk = kb.mark("eV")

            ustrip = AFP(98368, TOK)
            scscr2 = [AFP(102464, TOK), AFP(106560, TOK)]

            def f_p1(E):
                kb.W(E, "vector", gdma_mk)
                for c in range(DC):
                    kb.W(E, "vector", lam_evac_mks[(c, 1)])
                    E.tensor_scalar(out=ustrip, in0=lam_ch(c), scalar1=-1.0, scalar2=1.0,
                                    op0=ALU.mult, op1=ALU.add)
                    E.tensor_tensor(out=ustrip, in0=ustrip, in1=XN(c, 0, TOK), op=ALU.mult)
                    E.tensor_tensor_scan(out=scscr2[c % 2], data0=lam_ch(c), data1=ustrip,
                                         initial=0.0, op0=ALU.mult, op1=ALU.add)
                    if c >= 1:
                        kb.fin(E.tensor_copy(out=sfin[:, c - 1:c],
                                             in_=scscr2[(c - 1) % 2][:, TOK - 1:TOK]), "eV")
                E.tensor_copy(out=ustrip[:, 0:256], in_=ustrip[:, 256:512])
                kb.fin(E.tensor_copy(out=sfin[:, DC - 1:DC],
                                     in_=scscr2[(DC - 1) % 2][:, TOK - 1:TOK]), "eV")
            body("vector", f_p1)
            p1_mk = kb.mark("eV")

            xd_s = emit_exchange("s", 3, [(shr["pay_s"], sfin)], [(s_in, shr["pay_s"])],
                                 ready_mk=p1_mk, extra_mks=(consts_mk,))

            S0OFF = 131136
            S0B = ABF(S0OFF, DC * TOK)

            def s0_ch(c, h=None):
                if h is None:
                    return S0B[:, c * TOK:(c + 1) * TOK]
                return S0B[:, c * TOK + h * NT:c * TOK + (h + 1) * NT]

            def f_p2(E):
                kb.W(E, "vector", xd_s)
                kb.W(E, "vector", gred_mk)
                for c in range(DC):
                    E.tensor_scalar(out=ustrip, in0=lam_ch(c), scalar1=-1.0, scalar2=1.0,
                                    op0=ALU.mult, op1=ALU.add)
                    E.tensor_tensor(out=ustrip, in0=ustrip, in1=XN(c, 0, TOK), op=ALU.mult)
                    kb.fin(E.tensor_tensor_scan(out=s0_ch(c), data0=lam_ch(c),
                                                data1=ustrip, initial=s_in[:, c:c + 1],
                                                op0=ALU.mult, op1=ALU.add), "eV")
            body("vector", f_p2)
            p5_mk = kb.mark("eV")
            if debug:
                def f_dbg1(E):
                    kb.W(E, "gpsimd", p5_mk)
                    kb.W(E, "gpsimd", gamma_mk)
                    kb.dma(E, "gpsimd", "qG", dbg["dbg_lam"], LAM)
                    kb.dma(E, "gpsimd", "qG", dbg["dbg_tq"][:, 0:4], Tq)
                    kb.dma(E, "gpsimd", "qG", dbg["dbg_tq"][:, 4:8], gpart)
                    kb.dma(E, "gpsimd", "qG", dbg["dbg_tq"][:, 8:12], accg2)
                    kb.dma(E, "gpsimd", "qG", dbg["dbg_tq"][:, 12:16], accg)
                    kb.dma(E, "gpsimd", "qG", dbg["dbg_sin"][:, 0:DC], sfin)
                    kb.dma(E, "gpsimd", "qG", dbg["dbg_sin"][:, DC:2 * DC], s_in)
                    kb.dma(E, "gpsimd", "qG", dbg["dbg_gj"], gjd)
                    kb.dma(E, "gpsimd", "qG", dbg["dbg_rows"][0:1, 0:4], grow_t[:])
                    kb.dma(E, "gpsimd", "qG", dbg["dbg_rows"][0:1, 4:8], trow_t[:])
                body("gpsimd", f_dbg1)
                dbg1_mk = kb.mark("qG")
                p5_mk = ("qG", dbg1_mk[1])

            H0OFF = 32832
            H0B = ABF(H0OFF, H0C * TOK)

            def h0_ch(m, h=None):
                if m >= H0C:
                    raise IndexError
                if h is None:
                    return H0B[:, m * TOK:(m + 1) * TOK]
                return H0B[:, m * TOK + h * NT:m * TOK + (h + 1) * NT]

            sgst = [ABF(122944, NT), ABF(122944 + 1024, NT),
                    ABF(122944 + 2048, NT), ABF(122944 + 3072, NT)]

            w12jobs = []
            for m in range(H0C):
                w12jobs.append(d["w1p"][m])
                w12jobs.append(d["w2p"][m])
            sig_mks = {}
            h0_mks = {}

            def w12_evac(j, h, bidx, mm_mk):
                m, which = j // 2, j % 2
                if which == 0:
                    def f(E, m=m, h=h, bidx=bidx, mm_mk=mm_mk):
                        kb.W(E, "scalar", mm_mk)
                        if (m - 2, h) in h0_mks:
                            kb.W(E, "scalar", h0_mks[(m - 2, h)])
                        kb.fin(E.activation(out=sgst[2 * (m % 2) + h], in_=kb.pb[bidx],
                                            func=AF.Sigmoid), "eS")
                    body("scalar", f)
                    sig_mks[(m, h)] = kb.mark("eS")
                    return sig_mks[(m, h)]
                else:
                    def f(E, m=m, h=h, bidx=bidx, mm_mk=mm_mk):
                        kb.W(E, "vector", mm_mk)
                        kb.W(E, "vector", sig_mks[(m, h)])
                        kb.W(E, "vector", p5_mk)
                        kb.W(E, "vector", gred_mk)
                        kb.fin(E.scalar_tensor_tensor(out=h0_ch(m, h), in0=kb.pb[bidx],
                                                      scalar=1.0, in1=sgst[2 * (m % 2) + h],
                                                      op0=ALU.mult, op1=ALU.mult), "eV")
                    body("vector", f)
                    h0_mks[(m, h)] = kb.mark("eV")
                    return h0_mks[(m, h)]

            stream_panels(w12jobs, DC, 1, rhs_fn=lambda k, h: XN(k, h * NT, NT),
                          evac_fn=w12_evac, quant_col="pool_cmp",
                          pool_cols=lambda it: it % 2, cast_pre=(gamma_mk,))
            w12_mm_mk = kb.mark("eT")
            if debug:
                def f_dbg2(E):
                    kb.W(E, "gpsimd", w12_mm_mk)
                    kb.dma(E, "gpsimd", "qG", dbg["dbg_xn"], ABF(0, DC * XNSTR))
                body("gpsimd", f_dbg2)
                w12_mm_mk = ("qG", kb.mark("qG")[1])

            gst = [ABF(0, NT), ABF(1024, NT)]
            xrl = [AFP(2048, NT), AFP(2048 + 2048, NT)]
            t1s = AFP(6144, NT)
            onw = [AFP(8192, NT), AFP(8192 + 2048, NT)]
            cmb_mks = {}
            xrl_dma = {}

            def f_seed(E):
                kb.W(E, "gpsimd", w12_mm_mk)
                for i in range(2):
                    m, h = divmod(i, 2)
                    xrl_dma[i] = kb.rX.issue(
                        E, xrl[i % 2],
                        d["xin"][:, m * TOK + h * NT:m * TOK + (h + 1) * NT])
            body("gpsimd", f_seed)

            out_mks = {}

            def w3_evac(m, h, bidx, mm_mk):
                i = 2 * m + h

                def f_g(E, m=m, h=h, bidx=bidx, mm_mk=mm_mk, i=i):
                    kb.W(E, "scalar", mm_mk)
                    if i - 2 in cmb_mks:
                        kb.W(E, "scalar", cmb_mks[i - 2])
                    kb.fin(E.activation(out=gst[h], in_=kb.pb[bidx], func=AF.Silu,
                                        scale=0.5), "eS")
                body("scalar", f_g)
                g_mk = kb.mark("eS")

                def f_cmb(E, m=m, h=h, g_mk=g_mk, i=i):
                    kb.W(E, "vector", g_mk)
                    kb.W(E, "vector", kb.rX.mark(xrl_dma[i]))
                    if i - 2 in out_mks:
                        kb.W(E, "vector", out_mks[i - 2])
                    E.tensor_tensor(out=t1s, in0=s0_ch(m, h), in1=gst[h], op=ALU.mult)
                    kb.fin(E.tensor_tensor(out=onw[i % 2], in0=xrl[i % 2], in1=t1s,
                                           op=ALU.add), "eV")
                body("vector", f_cmb)
                cmb_mks[i] = kb.mark("eV")

                def f_out(E, m=m, h=h, i=i):
                    kb.W(E, "gpsimd", cmb_mks[i])
                    kb.dma(E, "gpsimd", "qG",
                           x1buf[:, m * TOK + h * NT:m * TOK + (h + 1) * NT], onw[i % 2])
                    if i + 2 < 2 * DC:
                        m2, h2 = divmod(i + 2, 2)
                        xrl_dma[i + 2] = kb.rX.issue(
                            E, xrl[i % 2],
                            d["xin"][:, m2 * TOK + h2 * NT:m2 * TOK + (h2 + 1) * NT])
                body("gpsimd", f_out)
                out_mks[i] = kb.mark("qG")
                return g_mk

            stream_panels([d["w3p"][m] for m in range(DC)], H0C, 4,
                          rhs_fn=lambda k, h: h0_ch(k, h),
                          evac_fn=w3_evac, quant_col=2, banks=(0, 1, 2, 3),
                          cast_pre=(h0_mks[(H0C - 1, 1)],))
            if debug:
                def f_dbg3(E):
                    h0last = max(h0_mks.values())[1] if False else None
                    kb.W(E, "gpsimd", kb.mark("eV"))
                    kb.dma(E, "gpsimd", "qG", dbg["dbg_s0"], S0B)
                    kb.dma(E, "gpsimd", "qG", dbg["dbg_h0"], H0B)
                body("gpsimd", f_dbg3)
            l0_out_mk = kb.mark("qG")

            if not with_l1:
                def f_fin(E):
                    kb.W(E, "gpsimd", l0_out_mk)
                body("gpsimd", f_fin)

        if with_l1:
            l1_start_mk = l0_out_mk if with_l0 else None

            def f_q0(E):
                for nm, dst in (("bkt", bk_sb), ("bvt", bv_sb), ("brt", br_sb),
                                ("bot", bo_sb), ("tmkt", tmk_sb), ("tmvt", tmv_sb),
                                ("tmrt", tmr_sb), ("ut", u_sb), ("wt", w_sb),
                                ("sugt", sug_sb), ("wsdpt", wsdp_sb)):
                    kb.dma(E, "scalar", "qA", dst, d[nm])
                kb.dma(E, "scalar", "qA", sdp_sb[:], d["sdpt"])
                kb.dma(E, "scalar", "qA", su2_sb[:], d["sut"])
                kb.dma(E, "scalar", "qA", sg_sb[:], d["sgt"])
                kb.dma(E, "scalar", "qA", h128_sb[:], d["h128"])
                if not with_l0:
                    kb.dma(E, "scalar", "qA", ctl_sb[:], d["ctl"])
            body("scalar", f_q0)
            q0_mk = kb.mark("qA")
            if not with_l0:
                ctl_mk = q0_mk

            xlast_b = sm(8).bitcast(BF16)
            xrecv_b = sm(8).bitcast(BF16)

            def f_q0v(E):
                kb.W(E, "vector", q0_mk)
                if not with_l0:
                    E.memset(ones_b[:], 1.0)
                    E.memset(onesf_col[:], 1.0)
                    E.memset(onesf_row[:], 1.0)
                    E.memset(flgsrc[:], 0)
                E.memset(xrecv_b, 0.0)
                E.memset(a_in, 0.0)
                E.memset(b_in, 0.0)
                E.tensor_scalar(out=nw_sb, in0=w_sb, scalar1=-1.0, scalar2=None,
                                op0=ALU.mult)
                E.tensor_tensor(out=uw_sb, in0=u_sb, in1=w_sb, op=ALU.add)
                kb.fin(E.tensor_tensor(out=su2_sb[:], in0=su2_sb[:], in1=sdp_sb[:],
                                       op=ALU.mult), "eV")
            body("vector", f_q0v)
            q0v_mk = kb.mark("eV")

            X1f = AFP(32832, DC * TOK)

            def x1ch(c, h=None):
                if h is None:
                    return X1f[:, c * TOK:(c + 1) * TOK]
                return X1f[:, c * TOK + h * NT:c * TOK + (h + 1) * NT]

            def f_x1l(E):
                if l1_start_mk is not None:
                    kb.W(E, "scalar", l1_start_mk)
                kb.dma(E, "scalar", "qA", X1f, x1buf)
            body("scalar", f_x1l)
            x1_mk = kb.mark("qA")

            xn1_done = {}
            for h in range(2):
                eS0 = kb.cnt["eS"]; eV0 = kb.cnt["eV"]; eT0 = kb.cnt["eT"]

                def f_sq(E, h=h, eT0=eT0):
                    kb.W(E, "scalar", x1_mk)
                    for c in range(DC):
                        if c >= 2:
                            E.wait_ge(sems["eT"], eT0 + c - 1)
                        E.square(out=sq_strips[c % 2], in_=x1ch(c, h)).then_inc(sems["eS"], 1)
                body("scalar", f_sq)
                kb.bump("eS", DC)

                def f_cp(E, h=h, eT0=eT0):
                    kb.W(E, "vector", x1_mk)
                    kb.W(E, "vector", q0v_mk)
                    for c in range(DC):
                        if c >= 2:
                            E.wait_ge(sems["eT"], eT0 + c - 1)
                        E.tensor_copy(out=xb_strips[c % 2], in_=x1ch(c, h)).then_inc(sems["eV"], 1)
                body("vector", f_cp)
                kb.bump("eV", DC)

                bA, bB = (0, 1) if h == 0 else (2, 3)

                def f_st(E, h=h, bA=bA, bB=bB, eS0=eS0, eV0=eV0):
                    bankA = kb.bank(E, "tensor", bA)
                    bankB = kb.bank(E, "tensor", bB)
                    for c in range(DC):
                        E.wait_ge(sems["eS"], eS0 + c + 1)
                        E.wait_ge(sems["eV"], eV0 + c + 1)
                        E.matmul(bankA[0:1, :], ones_b[:], xb_strips[c % 2],
                                 start=(c == 0), stop=(c == DC - 1))
                        E.matmul(bankB[0:1, :], ones_b[:], sq_strips[c % 2],
                                 start=(c == 0), stop=(c == DC - 1)).then_inc(sems["eT"], 1)
                        kb.bump("eT", 1)
                body("tensor", f_st)
                smm_mk = kb.mark("eT")

                def f_sf(E, bA=bA, bB=bB, smm_mk=smm_mk):
                    kb.W(E, "vector", smm_mk)
                    E.tensor_scalar(out=mrow, in0=kb.pb[bA][0:1, :], scalar1=1.0 / D,
                                    scalar2=None, op0=ALU.mult)
                    E.tensor_scalar(out=ssum, in0=kb.pb[bB][0:1, :], scalar1=1.0 / D,
                                    scalar2=None, op0=ALU.mult)
                    E.tensor_tensor(out=sqsum, in0=mrow, in1=mrow, op=ALU.mult)
                    E.tensor_tensor(out=ssum, in0=ssum, in1=sqsum, op=ALU.subtract)
                    kb.fin(E.tensor_scalar(out=ssum, in0=ssum, scalar1=LN_EPS,
                                           scalar2=None, op0=ALU.add), "eV")
                body("vector", f_sf)
                var_mk = kb.mark("eV")
                kb.bank_free[bA] = var_mk
                kb.bank_free[bB] = var_mk

                def f_sqr(E, var_mk=var_mk):
                    kb.W(E, "scalar", var_mk)
                    kb.fin(E.sqrt(out=sqsum, in_=ssum), "eS")
                body("scalar", f_sqr)
                sd_mk = kb.mark("eS")

                def f_rc(E, sd_mk=sd_mk):
                    kb.W(E, "vector", sd_mk)
                    kb.fin(E.reciprocal(out=rrow, in_=sqsum), "eV")
                body("vector", f_rc)
                rstd_mk = kb.mark("eV")

                bM, bR = (4, 5) if h == 0 else (6, 7)

                def f_bc(E, bM=bM, bR=bR, rstd_mk=rstd_mk):
                    kb.W(E, "tensor", rstd_mk)
                    bankM = kb.bank(E, "tensor", bM)
                    bankR = kb.bank(E, "tensor", bR)
                    E.matmul(bankM, onesf_row[:], mrow, start=True, stop=True)
                    kb.fin(E.matmul(bankR, onesf_row[:], rrow, start=True, stop=True), "eT")
                body("tensor", f_bc)
                bc_mk = kb.mark("eT")

                def f_xn(E, h=h, bM=bM, bR=bR, bc_mk=bc_mk):
                    kb.W(E, "vector", bc_mk)
                    for c in range(DC):
                        E.tensor_tensor(out=lntmp, in0=x1ch(c, h), in1=kb.pb[bM],
                                        op=ALU.subtract)
                        E.tensor_tensor(out=XN(c, h * NT, NT), in0=lntmp, in1=kb.pb[bR],
                                        op=ALU.mult)
                    kb.fin(E.engine_nop(), "eV")
                body("vector", f_xn)
                xn1_done[h] = kb.mark("eV")
                kb.bank_free[bM] = xn1_done[h]
                kb.bank_free[bR] = xn1_done[h]

            xn1_mk = xn1_done[1]

            def f_gather(E):
                kb.W(E, "vector", xn1_mk)
                xl_view = arena_b.tensor.ap().bitcast(BF16) if False else None
                src = arena_b[:, TOK:TOK + 1]
                lastv = arena_b[:, 0:DC * XNSTR].rearrange(
                    "p (c s) -> p c s", c=DC)[:, :, TOK:TOK + 1]
                col0v = arena_b[:, 0:DC * XNSTR].rearrange(
                    "p (c s) -> p c s", c=DC)[:, :, 0:1]
                E.tensor_copy(out=xlast_b, in_=lastv)
                E.memset(col0v, 0.0)
                E.tensor_copy(out=lntmp[:, 0:256], in_=lntmp[:, 256:512])
                kb.fin(E.engine_nop(), "eV")
            body("vector", f_gather)
            gather_mk = kb.mark("eV")

            xd_xn = emit_exchange("xn", 4, [(shr["pay_xn"], xlast_b)],
                                  [(xrecv_b, shr["pay_xn"])], ready_mk=gather_mk,
                                  extra_mks=(ctl_mk,))

            def f_col0(E):
                kb.W(E, "vector", xd_xn)
                col0v = arena_b[:, 0:DC * XNSTR].rearrange(
                    "p (c s) -> p c s", c=DC)[:, :, 0:1]
                E.tensor_copy(out=col0v, in_=xrecv_b)
                E.tensor_copy(out=lntmp[:, 0:256], in_=lntmp[:, 256:512])
                kb.fin(E.engine_nop(), "eV")
            body("vector", f_col0)
            col0_mk = kb.mark("eV")

            if debug:
                def f_d1(E):
                    kb.W(E, "gpsimd", col0_mk)
                    kb.dma(E, "gpsimd", "qG", dbg["dbg1_xn"], arena_b[:, 0:DC * XNSTR])
                body("gpsimd", f_d1)

            MIXOFF = 32832
            MIXB = ABF(MIXOFF, DC * TOK)
            KOFF, VOFF, ROFF = 65600, 98368, 131136
            KB_ = ABF(KOFF, DC * TOK)
            VB_ = ABF(VOFF, DC * TOK)
            RB_ = ABF(ROFF, DC * TOK)

            def mix_ch(c, h=None):
                if h is None:
                    return MIXB[:, c * TOK:(c + 1) * TOK]
                return MIXB[:, c * TOK + h * NT:c * TOK + (h + 1) * NT]

            def kch(c, h=None):
                if h is None:
                    return KB_[:, c * TOK:(c + 1) * TOK]
                return KB_[:, c * TOK + h * NT:c * TOK + (h + 1) * NT]

            def vch(c, h=None):
                if h is None:
                    return VB_[:, c * TOK:(c + 1) * TOK]
                return VB_[:, c * TOK + h * NT:c * TOK + (h + 1) * NT]

            def rch(c, h=None):
                if h is None:
                    return RB_[:, c * TOK:(c + 1) * TOK]
                return RB_[:, c * TOK + h * NT:c * TOK + (h + 1) * NT]

            dstrip = AFP(170048, TOK)
            rawstrip = AFP(174144, TOK)

            def emit_mix(tm_sb, presilu, prev_mm_mk):
                def f_mx(E):
                    kb.W(E, "vector", col0_mk)
                    if prev_mm_mk is not None:
                        kb.W(E, "vector", prev_mm_mk)
                    for c in range(DC):
                        xs = XN(c, -1, TOK)
                        xn_ = XN(c, 0, TOK)
                        E.tensor_tensor(out=dstrip, in0=xn_, in1=xs, op=ALU.subtract)
                        if presilu:
                            E.scalar_tensor_tensor(out=rawstrip, in0=dstrip,
                                                   scalar=tm_sb[:, c:c + 1], in1=xs,
                                                   op0=ALU.mult, op1=ALU.add
                                                   ).then_inc(sems["eV"], 1)
                            kb.bump("eV", 1)
                        else:
                            E.scalar_tensor_tensor(out=mix_ch(c), in0=dstrip,
                                                   scalar=tm_sb[:, c:c + 1], in1=xs,
                                                   op0=ALU.mult, op1=ALU.add
                                                   ).then_inc(sems["eV"], 1)
                            kb.bump("eV", 1)
                        if presilu:
                            mkv = kb.mark("eV")

                            def f_si(E2, c=c, mkv=mkv):
                                kb.W(E2, "scalar", mkv)
                                kb.fin(E2.activation(out=mix_ch(c), in_=rawstrip,
                                                     func=AF.Silu), "eS")
                            body("scalar", f_si)
                    kb.fin(E.engine_nop(), "eV")
                if not presilu:
                    body("vector", f_mx)
                    return kb.mark("eV")
                else:
                    raise RuntimeError("use emit_mix_silu")

            def emit_mix_silu(tm_sb, prev_mm_mk):
                raw_mks = {}

                def f_mx(E):
                    kb.W(E, "vector", col0_mk)
                    if prev_mm_mk is not None:
                        kb.W(E, "vector", prev_mm_mk)
                    for c in range(DC):
                        if c >= 1:
                            kb.W(E, "vector", raw_mks[c - 1])
                        xs = XN(c, -1, TOK)
                        xn_ = XN(c, 0, TOK)
                        E.tensor_tensor(out=dstrip, in0=xn_, in1=xs, op=ALU.subtract)
                        E.scalar_tensor_tensor(out=rawstrip, in0=dstrip,
                                               scalar=tm_sb[:, c:c + 1], in1=xs,
                                               op0=ALU.mult, op1=ALU.add
                                               ).then_inc(sems["eV"], 1)
                        kb.bump("eV", 1)
                        mkv = kb.mark("eV")

                        def f_si(E2, c=c, mkv=mkv):
                            kb.W(E2, "scalar", mkv)
                            kb.fin(E2.activation(out=mix_ch(c), in_=rawstrip,
                                                 func=AF.Silu), "eS")
                        raw_mks[c] = ("eS", kb.cnt["eS"] + 1)
                        body_later.append(("scalar", f_si))
                    kb.fin(E.engine_nop(), "eV")
                raise RuntimeError("unused")

            def emit_mix2(tm_sb, presilu, prev_mm_mk):
                raws = [AFP(174144, TOK), AFP(178240, TOK)]
                mks = {}
                for c in range(DC):
                    def f_v(E, c=c):
                        if c == 0:
                            kb.W(E, "vector", col0_mk)
                            if prev_mm_mk is not None:
                                kb.W(E, "vector", prev_mm_mk)
                        if presilu and c >= 2:
                            kb.W(E, "vector", mks[c - 2])
                        xs = XN(c, -1, TOK)
                        xn_ = XN(c, 0, TOK)
                        E.tensor_tensor(out=dstrip, in0=xn_, in1=xs, op=ALU.subtract)
                        tgt = raws[c % 2] if presilu else mix_ch(c)
                        kb.fin(E.scalar_tensor_tensor(out=tgt, in0=dstrip,
                                                      scalar=tm_sb[:, c:c + 1], in1=xs,
                                                      op0=ALU.mult, op1=ALU.add), "eV")
                    body("vector", f_v)
                    vmk = kb.mark("eV")
                    if presilu:
                        def f_s(E, c=c, vmk=vmk):
                            kb.W(E, "scalar", vmk)
                            kb.fin(E.activation(out=mix_ch(c), in_=raws[c % 2],
                                                func=AF.Silu), "eS")
                        body("scalar", f_s)
                        mks[c] = kb.mark("eS")
                    else:
                        mks[c] = vmk
                return mks[DC - 1]

            def proj_evac(dstf, bias_sb):
                mks = {}

                def evac(m, h, bidx, mm_mk):
                    def f(E, m=m, h=h, bidx=bidx, mm_mk=mm_mk):
                        kb.W(E, "vector", mm_mk)
                        kb.fin(E.tensor_scalar(out=dstf(m, h), in0=kb.pb[bidx],
                                               scalar1=bias_sb[:, m:m + 1],
                                               scalar2=None, op0=ALU.add), "eV")
                    body("vector", f)
                    mks[(m, h)] = kb.mark("eV")
                    return mks[(m, h)]
                return evac, mks

            kmix_mk = emit_mix2(tmk_sb, False, None)
            k_evac, k_mks = proj_evac(kch, bk_sb)
            stream_panels([d["wkp"][m] for m in range(DC)], DC, 4,
                          rhs_fn=lambda k, h: mix_ch(k, h), evac_fn=k_evac,
                          cast_pre=(kmix_mk,), banks=(0, 1, 2, 3), cast_eng="gpsimd")
            wk_mm_mk = kb.stream_guard[1]
            vmix_mk = emit_mix2(tmv_sb, False, wk_mm_mk)
            v_evac, v_mks = proj_evac(vch, bv_sb)
            stream_panels([d["wvp"][m] for m in range(DC)], DC, 4,
                          rhs_fn=lambda k, h: mix_ch(k, h), evac_fn=v_evac,
                          cast_pre=(vmix_mk,), banks=(0, 1, 2, 3), cast_eng="gpsimd")
            wv_mm_mk = kb.stream_guard[1]
            rmix_mk = emit_mix2(tmr_sb, True, wv_mm_mk)
            r_evac, r_mks = proj_evac(rch, br_sb)
            stream_panels([d["wrp"][m] for m in range(DC)], DC, 4,
                          rhs_fn=lambda k, h: mix_ch(k, h), evac_fn=r_evac,
                          cast_pre=(rmix_mk,), banks=(0, 1, 2, 3), cast_eng="gpsimd")
            wr_mm_mk = kb.stream_guard[1]
            dbg_q3_mk = None
            if debug:
                def f_d2(E):
                    kb.W(E, "gpsimd", k_mks[(DC - 1, 1)])
                    kb.W(E, "gpsimd", v_mks[(DC - 1, 1)])
                    kb.W(E, "gpsimd", r_mks[(DC - 1, 1)])
                    kb.dma(E, "gpsimd", "qG", dbg["dbg1_k"], KB_)
                    kb.dma(E, "gpsimd", "qG", dbg["dbg1_v"], VB_)
                    kb.dma(E, "gpsimd", "qG", dbg["dbg1_r"], RB_)
                body("gpsimd", f_d2)
                dbg_q3_mk = kb.mark("qG")

            tts = [AFP(32832, TOK), AFP(36928, TOK)]
            e1s = [AFP(41024, TOK), AFP(45120, TOK)]
            ex2s = [AFP(49216, TOK), AFP(53312, TOK)]
            sa = AFP(57408, TOK)
            sb_ = AFP(61504, TOK)

            def emit_scan_pass(pass2):
                va_mks = {}
                s_mks = {}

                def VA(E, c):
                    st = c % 2
                    if c == 0:
                        kb.W(E, "vector", wr_mm_mk)
                    if pass2 and c == 0:
                        kb.W(E, "vector", xd_ab)
                        if dbg_q3_mk is not None:
                            kb.W(E, "vector", dbg_q3_mk)
                    kb.W(E, "vector", k_mks[(c, 1)])
                    kb.fin(E.tensor_scalar(out=tts[st], in0=kch(c),
                                           scalar1=u_sb[:, c:c + 1],
                                           scalar2=w_sb[:, c:c + 1],
                                           op0=ALU.add, op1=ALU.max), "eV")

                def S(E, c):
                    st = c % 2
                    kb.W(E, "scalar", va_mks[c])
                    E.activation(out=e1s[st], in_=tts[st], func=AF.Exp,
                                 scale=-1.0, bias=nw_sb[:, c:c + 1])
                    kb.fin(E.activation(out=ex2s[st], in_=kch(c), func=AF.Exp,
                                        bias=uw_sb[:, c:c + 1]), "eS")

                def VC(E, c):
                    st = c % 2
                    kb.W(E, "vector", s_mks[c])
                    kb.W(E, "vector", v_mks[(c, 1)])
                    E.tensor_tensor(out=ex2s[st], in0=e1s[st], in1=ex2s[st],
                                    op=ALU.mult)
                    E.tensor_tensor(out=tts[st], in0=ex2s[st], in1=vch(c),
                                    op=ALU.mult)
                    ia = a_in[:, c:c + 1] if pass2 else 0.0
                    ib = b_in[:, c:c + 1] if pass2 else 0.0
                    E.tensor_tensor_scan(out=sa, data0=e1s[st], data1=tts[st],
                                         initial=ia, op0=ALU.mult, op1=ALU.add)
                    E.tensor_tensor_scan(out=sb_, data0=e1s[st], data1=ex2s[st],
                                         initial=ib, op0=ALU.mult, op1=ALU.add)
                    if not pass2:
                        E.tensor_copy(out=afin[:, c:c + 1], in_=sa[:, TOK - 1:TOK])
                        E.tensor_copy(out=lntmp[:, 0:256], in_=lntmp[:, 256:512])
                        E.tensor_copy(out=bfin[:, c:c + 1], in_=sb_[:, TOK - 1:TOK])
                        kb.fin(E.tensor_copy(out=lntmp[:, 256:512],
                                             in_=lntmp[:, 0:256]), "eV")
                    else:
                        kb.W(E, "vector", r_mks[(c, 1)])
                        E.reciprocal(out=ex2s[st], in_=sb_)
                        E.tensor_tensor(out=sa, in0=sa, in1=ex2s[st], op=ALU.mult)
                        kb.fin(E.tensor_tensor(out=rch(c), in0=sa, in1=rch(c),
                                               op=ALU.mult), "eV")

                for c in range(DC):
                    def f_va(E, c=c):
                        VA(E, c)
                    body("vector", f_va)
                    va_mks[c] = kb.mark("eV")

                    def f_s(E, c=c):
                        S(E, c)
                    body("scalar", f_s)
                    s_mks[c] = kb.mark("eS")
                    if c >= 1:
                        def f_vc(E, c=c):
                            VC(E, c - 1)
                        body("vector", f_vc)
                def f_vcl(E):
                    VC(E, DC - 1)
                body("vector", f_vcl)
                return kb.mark("eV")

            p1ab_mk = emit_scan_pass(False)
            xd_ab = emit_exchange("ab", 5,
                                  [(shr["pay_ab"][:, 0:DC], afin),
                                   (shr["pay_ab"][:, DC:2 * DC], bfin)],
                                  [(a_in, shr["pay_ab"][:, 0:DC]),
                                   (b_in, shr["pay_ab"][:, DC:2 * DC])],
                                  ready_mk=p1ab_mk, extra_mks=(ctl_mk,))
            wkvr_mk = emit_scan_pass(True)
            dbg_q4_mk = None
            if debug:
                def f_d3(E):
                    kb.W(E, "gpsimd", wkvr_mk)
                    kb.dma(E, "gpsimd", "qG", dbg["dbg1_ab"][:, 0:DC], afin)
                    kb.dma(E, "gpsimd", "qG", dbg["dbg1_ab"][:, DC:2 * DC], bfin)
                    kb.dma(E, "gpsimd", "qG", dbg["dbg1_ab"][:, 2 * DC:3 * DC], a_in)
                    kb.dma(E, "gpsimd", "qG", dbg["dbg1_ab"][:, 3 * DC:4 * DC], b_in)
                    kb.dma(E, "gpsimd", "qG", dbg["dbg1_wkvr"], RB_)
                body("gpsimd", f_d3)
                dbg_q4_mk = kb.mark("qG")

            s1st = [ABF(170048, NT), ABF(170048 + 1024, NT)]
            s1_out_mks = {}

            def wo_evac(m, h, bidx, mm_mk):
                i = 2 * m + h

                def f(E, m=m, h=h, bidx=bidx, mm_mk=mm_mk, i=i):
                    kb.W(E, "vector", mm_mk)
                    kb.W(E, "vector", wkvr_mk)
                    if i - 2 in s1_out_mks:
                        kb.W(E, "vector", s1_out_mks[i - 2])
                    kb.fin(E.tensor_scalar(out=s1st[i % 2], in0=kb.pb[bidx],
                                           scalar1=bo_sb[:, m:m + 1], scalar2=None,
                                           op0=ALU.add), "eV")
                body("vector", f)
                ev = kb.mark("eV")

                def f_o(E, m=m, h=h, ev=ev, i=i):
                    kb.W(E, "gpsimd", ev)
                    kb.dma(E, "gpsimd", "qG",
                           s1sp[:, m * TOK + h * NT:m * TOK + (h + 1) * NT], s1st[i % 2])
                body("gpsimd", f_o)
                s1_out_mks[i] = kb.mark("qG")
                return ev

            stream_panels([d["wop"][m] for m in range(DC)], DC, 4,
                          rhs_fn=lambda k, h: rch(k, h), evac_fn=wo_evac,
                          cast_pre=(wkvr_mk,), banks=(0, 1, 2, 3))
            wo_done_mk = kb.mark("qG")

            ZOFF = 163904
            ZB = ABF(ZOFF, DC * TOK)

            def zch(c, h=None):
                if h is None:
                    return ZB[:, c * TOK:(c + 1) * TOK]
                return ZB[:, c * TOK + h * NT:c * TOK + (h + 1) * NT]

            FA, FB = 65600, 98368
            z_done_mks = {}
            zprev = {}
            for st_i in range(2):
                sA = AFP(FA, DC * NT)
                sB = AFP(FB, DC * NT)

                def f_su(E, st_i=st_i, sA=sA):
                    kb.W(E, "vector", wo_done_mk)
                    kb.W(E, "vector", xn1_mk)
                    if "cst" in zprev:
                        kb.W(E, "vector", zprev["cst"])
                    if "evacs" in zprev:
                        kb.W(E, "vector", zprev["evacs"])
                    if "dump" in zprev:
                        kb.W(E, "vector", zprev["dump"])
                    for c in range(DC):
                        E.scalar_tensor_tensor(out=sA[:, c * NT:(c + 1) * NT],
                                               in0=XN(c, st_i * NT, NT),
                                               scalar=sug_sb[:, c:c + 1],
                                               in1=XN(c, st_i * NT, NT),
                                               op0=ALU.mult, op1=ALU.bypass) \
                            if False else \
                            E.tensor_scalar(out=sA[:, c * NT:(c + 1) * NT],
                                            in0=XN(c, st_i * NT, NT),
                                            scalar1=sug_sb[:, c:c + 1], scalar2=None,
                                            op0=ALU.mult)
                    kb.fin(E.engine_nop(), "eV")
                body("vector", f_su)

                bufs = [sA, sB]
                for stg in range(4):
                    src, dst = bufs[stg % 2], bufs[(stg + 1) % 2]
                    stride = 1 << stg

                    def f_bf(E, src=src, dst=dst, stride=stride):
                        g = DC // (2 * stride)
                        s4 = src.rearrange("p (g two r s) -> p g two r s",
                                           g=g, two=2, r=stride)
                        d4 = dst.rearrange("p (g two r s) -> p g two r s",
                                           g=g, two=2, r=stride)
                        E.tensor_tensor(out=d4[:, :, 0], in0=s4[:, :, 0],
                                        in1=s4[:, :, 1], op=ALU.add)
                        E.tensor_tensor(out=d4[:, :, 1], in0=s4[:, :, 0],
                                        in1=s4[:, :, 1], op=ALU.subtract)
                        kb.fin(E.engine_nop(), "eV")
                    body("vector", f_bf)
                bfly_mk = kb.mark("eV")
                sAb = ABF(FA, DC * NT)

                def f_cst(E, sA=sA, sAb=sAb, bfly_mk=bfly_mk):
                    kb.W(E, "scalar", bfly_mk)
                    kb.fin(E.copy(out=sAb, in_=sA), "eS")
                sBb = ABF(FB, DC * NT)

                def f_cst2(E, sA=sA, sBb=sBb, bfly_mk=bfly_mk):
                    kb.W(E, "scalar", bfly_mk)
                    if "mm_last" in zprev:
                        kb.W(E, "scalar", zprev["mm_last"])
                    kb.fin(E.copy(out=sBb, in_=sA), "eS")
                body("scalar", f_cst2)
                cst_mk = kb.mark("eS")
                zprev["cst"] = cst_mk

                for c in range(DC):
                    bidx = c % 4

                    def f_mm(E, c=c, bidx=bidx, cst_mk=cst_mk, sBb=sBb):
                        kb.W(E, "tensor", cst_mk)
                        kb.W(E, "tensor", q0_mk)
                        bank = kb.bank(E, "tensor", bidx)
                        kb.fin(E.matmul(bank, h128_sb[:],
                                        sBb[:, c * NT:(c + 1) * NT],
                                        start=True, stop=True), "eT")
                    body("tensor", f_mm)
                    mm_mk = kb.mark("eT")

                    def f_ev(E, c=c, st_i=st_i, bidx=bidx, mm_mk=mm_mk):
                        kb.W(E, "scalar", mm_mk)
                        kb.fin(E.activation(out=zch(c, st_i), in_=kb.pb[bidx],
                                            func=AF.Copy,
                                            scale=1.0 / float(np.sqrt(D))), "eS")
                    body("scalar", f_ev)
                    z_done_mks[(c, st_i)] = kb.mark("eS")
                    kb.bank_free[bidx] = z_done_mks[(c, st_i)]
                zprev["mm_last"] = kb.mark("eT")
                zprev["evacs"] = kb.mark("eS")
                if debug and st_i == 0:
                    def f_dsb(E, sBb=sBb):
                        kb.W(E, "gpsimd", kb.mark("eS"))
                        kb.dma(E, "gpsimd", "qG", dbg["dbg1_sb"], sBb)
                    body("gpsimd", f_dsb)
                    zprev["dump"] = kb.mark("qG")
            z_mk = kb.mark("eS")
            if debug:
                def f_dz2(E):
                    kb.W(E, "gpsimd", z_mk)
                    kb.dma(E, "gpsimd", "qG", dbg["dbg1_z2"], ZB)
                body("gpsimd", f_dz2)

            HOFF = 32832
            HB1 = ABF(HOFF, H1C * TOK)

            def hch(m, h=None):
                if h is None:
                    return HB1[:, m * TOK:(m + 1) * TOK]
                return HB1[:, m * TOK + h * NT:m * TOK + (h + 1) * NT]

            gsst = [ABF(24576, NT), ABF(24576 + 1024, NT),
                    ABF(24576 + 2048, NT), ABF(24576 + 3072, NT)]
            ug_jobs = []
            for m in range(H1C):
                ug_jobs.append(d["wgp"][m])
                ug_jobs.append(d["wup"][m])
            gs_mks = {}
            h1_mks = {}

            def ug_evac(j, h, bidx, mm_mk):
                m, which = j // 2, j % 2
                if which == 0:
                    def f(E, m=m, h=h, bidx=bidx, mm_mk=mm_mk):
                        kb.W(E, "scalar", mm_mk)
                        kb.W(E, "scalar", z_mk)
                        if (m - 2, h) in h1_mks:
                            kb.W(E, "scalar", h1_mks[(m - 2, h)])
                        kb.fin(E.activation(out=gsst[2 * (m % 2) + h], in_=kb.pb[bidx],
                                            func=AF.Silu,
                                            scale=sg_sb[:, m:m + 1]), "eS")
                    body("scalar", f)
                    gs_mks[(m, h)] = kb.mark("eS")
                    return gs_mks[(m, h)]
                else:
                    def f(E, m=m, h=h, bidx=bidx, mm_mk=mm_mk):
                        kb.W(E, "vector", mm_mk)
                        kb.W(E, "vector", gs_mks[(m, h)])
                        if dbg_q4_mk is not None:
                            kb.W(E, "vector", dbg_q4_mk)
                        kb.fin(E.scalar_tensor_tensor(out=hch(m, h), in0=kb.pb[bidx],
                                                      scalar=su2_sb[:, m:m + 1],
                                                      in1=gsst[2 * (m % 2) + h],
                                                      op0=ALU.mult, op1=ALU.mult), "eV")
                    body("vector", f)
                    h1_mks[(m, h)] = kb.mark("eV")
                    return h1_mks[(m, h)]

            ug_dma_pre = [z_mk]
            if debug:
                ug_dma_pre += [dbg_q3_mk, ("qG", kb.cnt["qG"])]
            stream_panels(ug_jobs, DC, 1, rhs_fn=lambda k, h: zch(k, h),
                          evac_fn=ug_evac, cast_pre=(z_mk,), dma_pre=tuple(ug_dma_pre),
                          str_off=0, str_cap=24576, cast_eng="gpsimd")
            ug_mm_mk = kb.stream_guard[1]
            dbg_q7_mk = None
            if debug:
                def f_d4(E):
                    kb.W(E, "gpsimd", z_mk)
                    kb.W(E, "gpsimd", h1_mks[(H1C - 1, 1)])
                    kb.W(E, "gpsimd", wo_done_mk)
                    kb.dma(E, "gpsimd", "qG", dbg["dbg1_z"], ZB)
                    kb.dma(E, "gpsimd", "qG", dbg["dbg1_h"], HB1)
                    kb.dma(E, "gpsimd", "qG", dbg["dbg1_s1"], s1sp)
                body("gpsimd", f_d4)
                dbg_q7_mk = kb.mark("qG")

            TS = 128
            NSTRIP = TOK // TS
            BFA, BFB = 163904, 180288
            bf_last = None
            for si in range(NSTRIP):
                sA = ABF(BFA, H1C * TS)
                sB = ABF(BFB, H1C * TS)
                hsv = HB1.rearrange("p (c t) -> p c t", c=H1C)[:, :, si * TS:(si + 1) * TS]

                def f_in(E, si=si, sA=sA, hsv=hsv):
                    if si == 0:
                        kb.W(E, "vector", ug_mm_mk)
                        kb.W(E, "vector", wo_done_mk)
                        if dbg_q7_mk is not None:
                            kb.W(E, "vector", dbg_q7_mk)
                    g = H1C // 2
                    s4 = hsv.rearrange("p (g two r) t -> p g two r t", g=g, two=2, r=1) \
                        if False else hsv.rearrange("p (g two r) t -> p g two r t",
                                                    g=H1C // 2, two=2, r=1)
                    a4 = sA.rearrange("p (g two r s) -> p g two r s",
                                      g=H1C // 2, two=2, r=1)
                    E.tensor_tensor(out=a4[:, :, 0], in0=s4[:, :, 0], in1=s4[:, :, 1],
                                    op=ALU.add)
                    E.tensor_tensor(out=a4[:, :, 1], in0=s4[:, :, 0], in1=s4[:, :, 1],
                                    op=ALU.subtract)
                    kb.fin(E.engine_nop(), "eV")
                body("vector", f_in)

                bufs = [sA, sB]
                for stg in range(1, 6):
                    src, dst = bufs[(stg - 1) % 2], bufs[stg % 2]
                    stride = 1 << stg

                    def f_bf(E, src=src, dst=dst, stride=stride):
                        g = H1C // (2 * stride)
                        s4 = src.rearrange("p (g two r s) -> p g two r s",
                                           g=g, two=2, r=stride)
                        d4 = dst.rearrange("p (g two r s) -> p g two r s",
                                           g=g, two=2, r=stride)
                        E.tensor_tensor(out=d4[:, :, 0], in0=s4[:, :, 0],
                                        in1=s4[:, :, 1], op=ALU.add)
                        E.tensor_tensor(out=d4[:, :, 1], in0=s4[:, :, 0],
                                        in1=s4[:, :, 1], op=ALU.subtract)
                        kb.fin(E.engine_nop(), "eV")
                    body("vector", f_bf)

                def f_wb(E, si=si, sB=sB, hsv=hsv):
                    kb.fin(E.tensor_copy(out=hsv, in_=sB.rearrange(
                        "p (c s) -> p c s", c=H1C)), "eV")
                body("vector", f_wb)
                bf_last = kb.mark("eV")

            h128_done = {}
            for c in range(H1C):
                for h in range(2):
                    bidx = (2 * c + h) % 4

                    def f_mm(E, c=c, h=h, bidx=bidx):
                        kb.W(E, "tensor", bf_last)
                        bank = kb.bank(E, "tensor", bidx)
                        kb.fin(E.matmul(bank, h128_sb[:], hch(c, h),
                                        start=True, stop=True), "eT")
                    body("tensor", f_mm)
                    mm_mk = kb.mark("eT")

                    def f_ev(E, c=c, h=h, bidx=bidx, mm_mk=mm_mk):
                        kb.W(E, "scalar", mm_mk)
                        kb.fin(E.activation(out=hch(c, h), in_=kb.pb[bidx],
                                            func=AF.Copy,
                                            scale=1.0 / float(np.sqrt(H1))), "eS")
                    body("scalar", f_ev)
                    h128_done[(c, h)] = kb.mark("eS")
                    kb.bank_free[bidx] = h128_done[(c, h)]
            hh_mk = kb.mark("eS")
            if debug:
                def f_d5(E):
                    kb.W(E, "gpsimd", hh_mk)
                    kb.dma(E, "gpsimd", "qG", dbg["dbg1_hh"], HB1)
                body("gpsimd", f_d5)
                hh_mk = ("qG", kb.mark("qG")[1])

            gdst = [ABF(163904, NT), ABF(163904 + 1024, NT)]
            x1rl = [AFP(166016, NT), AFP(166016 + 2048, NT)]
            s1rl = [ABF(170112, NT), ABF(170112 + 1024, NT)]
            t2s = AFP(172160, TOK // 2)
            onw2 = [AFP(174208, NT), AFP(174208 + 2048, NT)]
            cmb2_mks = {}
            yout_mks = {}
            rl_dma = {}

            def f_seed2(E):
                kb.W(E, "gpsimd", hh_mk)
                kb.W(E, "gpsimd", wo_done_mk)
                for i in range(2):
                    m, h = divmod(i, 2)
                    rl_dma[(i, 0)] = kb.rX.issue(
                        E, x1rl[i % 2],
                        x1buf[:, m * TOK + h * NT:m * TOK + (h + 1) * NT])
                    rl_dma[(i, 1)] = kb.rX.issue(
                        E, s1rl[i % 2],
                        s1sp[:, m * TOK + h * NT:m * TOK + (h + 1) * NT])
            body("gpsimd", f_seed2)

            def wd_evac(m, h, bidx, mm_mk):
                i = 2 * m + h

                def f_g(E, m=m, h=h, bidx=bidx, mm_mk=mm_mk, i=i):
                    kb.W(E, "scalar", mm_mk)
                    if i - 2 in cmb2_mks:
                        kb.W(E, "scalar", cmb2_mks[i - 2])
                    kb.fin(E.activation(out=gdst[h], in_=kb.pb[bidx], func=AF.Silu,
                                        scale=wsdp_sb[:, m:m + 1]), "eS")
                body("scalar", f_g)
                g_mk = kb.mark("eS")

                def f_c(E, m=m, h=h, g_mk=g_mk, i=i):
                    kb.W(E, "vector", g_mk)
                    kb.W(E, "vector", kb.rX.mark(rl_dma[(i, 0)]))
                    kb.W(E, "vector", kb.rX.mark(rl_dma[(i, 1)]))
                    if i - 2 in yout_mks:
                        kb.W(E, "vector", yout_mks[i - 2])
                    E.tensor_tensor(out=t2s, in0=s1rl[i % 2], in1=gdst[h], op=ALU.mult)
                    kb.fin(E.tensor_tensor(out=onw2[i % 2], in0=x1rl[i % 2], in1=t2s,
                                           op=ALU.add), "eV")
                body("vector", f_c)
                cmb2_mks[i] = kb.mark("eV")

                def f_o(E, m=m, h=h, i=i):
                    kb.W(E, "gpsimd", cmb2_mks[i])
                    kb.dma(E, "gpsimd", "qG",
                           yout[:, m * TOK + h * NT:m * TOK + (h + 1) * NT], onw2[i % 2])
                    if i + 2 < 2 * DC:
                        m2, h2 = divmod(i + 2, 2)
                        rl_dma[(i + 2, 0)] = kb.rX.issue(
                            E, x1rl[i % 2],
                            x1buf[:, m2 * TOK + h2 * NT:m2 * TOK + (h2 + 1) * NT])
                        rl_dma[(i + 2, 1)] = kb.rX.issue(
                            E, s1rl[i % 2],
                            s1sp[:, m2 * TOK + h2 * NT:m2 * TOK + (h2 + 1) * NT])
                body("gpsimd", f_o)
                yout_mks[i] = kb.mark("qG")
                return g_mk

            stream_panels([d["wdp"][m] for m in range(DC)], H1C, 4,
                          rhs_fn=lambda k, h: hch(k, h), evac_fn=wd_evac,
                          banks=(0, 1, 2, 3), cast_pre=(hh_mk,), str_off=0, str_cap=24576)
            l1_out_mk = kb.mark("qG")

            def f_fin1(E):
                kb.W(E, "gpsimd", l1_out_mk)
            body("gpsimd", f_fin1)

    return nc



def _mk_ctl(c, nonces):
    b, h = divmod(c, 2)
    ctl = np.zeros((1, 12), np.int32)
    ctl[0, 0] = 1 if h == 0 else 0
    ctl[0, 1] = 1 if h == 1 else 0
    ctl[0, 2:2 + len(nonces)] = nonces
    return ctl


def _prep_l0(inputs, nonces):
    shared = {
        "w0p": pack_panels(_f32(inputs["L0_attn_W"]).T),
        "w1p": pack_panels(_f32(inputs["L0_ffn_w1"]).T),
        "w2p": pack_panels(_f32(inputs["L0_ffn_w2"]).T),
        "w3p": pack_panels(_f32(inputs["L0_ffn_w3"]).T),
        "b0t": pack_vec(inputs["L0_attn_b"]),
    }
    x = _f32(inputs["L0_attn_x"])
    maps = []
    for c in range(8):
        b, h = divmod(c, 2)
        m = dict(shared)
        m["xin"] = pack_x_shard(x[b, h * TOK:(h + 1) * TOK, :])
        m["ctl"] = _mk_ctl(c, nonces)
        maps.append(m)
    return maps


def run_l0(inputs, debug=False):
    nonces = np.random.randint(1, 2 ** 30, size=4).astype(np.int32)
    nc = build_program(with_l0=True, with_l1=False, debug=debug)
    maps = _prep_l0(inputs, nonces)
    res = run_bass_kernel_spmd(nc, maps, core_ids=list(range(8)))
    out = np.empty((B, T, D), np.float32)
    for c in range(8):
        bb, h = divmod(c, 2)
        out[bb, h * TOK:(h + 1) * TOK, :] = unpack_x_shard(res.results[c]["x1out"])
    if debug:
        return out, [dict(r) for r in res.results]
    return out


def _prep_l1(inputs, nonces, x1_packed=None):
    shared = {
        "wkp": pack_panels(_f32(inputs["L1_attn_Wk"]).T),
        "wvp": pack_panels(_f32(inputs["L1_attn_Wv"]).T),
        "wrp": pack_panels(_f32(inputs["L1_attn_Wr"]).T),
        "wop": pack_panels(_f32(inputs["L1_attn_Wo"]).T),
        "wup": pack_panels(_f32(inputs["L1_ffn_W_up"]).T),
        "wgp": pack_panels(_f32(inputs["L1_ffn_W_gate"]).T),
        "wdp": pack_panels(_f32(inputs["L1_ffn_W_down"]).T),
        "bkt": pack_vec(inputs["L1_attn_bk"]),
        "bvt": pack_vec(inputs["L1_attn_bv"]),
        "brt": pack_vec(inputs["L1_attn_br"]),
        "bot": pack_vec(inputs["L1_attn_bo"]),
        "tmkt": pack_vec(inputs["L1_attn_tmk"]),
        "tmvt": pack_vec(inputs["L1_attn_tmv"]),
        "tmrt": pack_vec(inputs["L1_attn_tmr"]),
        "ut": pack_vec(inputs["L1_attn_u"]),
        "wt": pack_vec(inputs["L1_attn_w"]),
        "sugt": pack_vec(inputs["L1_ffn_SU_ug"]),
        "sdpt": pack_vec(inputs["L1_ffn_SU_dp"]),
        "sut": pack_vec(np.asarray(inputs["L1_ffn_Wscale_ug"], np.float32)[:H1]),
        "sgt": pack_vec(np.asarray(inputs["L1_ffn_Wscale_ug"], np.float32)[H1:]),
        "wsdpt": pack_vec(inputs["L1_ffn_Wscale_dp"]),
        "h128": np.ascontiguousarray(hadamard128_bf16()),
    }
    maps = []
    for c in range(8):
        b, h = divmod(c, 2)
        m = dict(shared)
        if x1_packed is not None:
            m["x1in"] = x1_packed[c]
        m["ctl"] = _mk_ctl(c, nonces)
        maps.append(m)
    return maps


def run_l1(inputs, x1, debug=False):
    nonces = np.random.randint(1, 2 ** 30, size=4).astype(np.int32)
    nc = build_program(with_l0=False, with_l1=True, debug=debug)
    x1p = []
    for c in range(8):
        b, h = divmod(c, 2)
        x1p.append(pack_x_shard(_f32(x1)[b, h * TOK:(h + 1) * TOK, :]))
    maps = _prep_l1(inputs, nonces, x1p)
    res = run_bass_kernel_spmd(nc, maps, core_ids=list(range(8)))
    out = np.empty((B, T, D), np.float32)
    for c in range(8):
        bb, h = divmod(c, 2)
        out[bb, h * TOK:(h + 1) * TOK, :] = unpack_x_shard(res.results[c]["yout"])
    if debug:
        return out, [dict(r) for r in res.results]
    return out


FUSED = True


def kernel(**inputs):
    if not FUSED:
        nonces0 = np.random.randint(1, 2 ** 30, size=4).astype(np.int32)
        nc0 = build_program(with_l0=True, with_l1=False)
        maps0 = _prep_l0(inputs, nonces0)
        res0 = run_bass_kernel_spmd(nc0, maps0, core_ids=list(range(8)))
        x1p = [np.ascontiguousarray(np.asarray(res0.results[c]["x1out"]))
               for c in range(8)]
        nonces1 = np.random.randint(1, 2 ** 30, size=4).astype(np.int32)
        nc1 = build_program(with_l0=False, with_l1=True)
        maps1 = _prep_l1(inputs, nonces1, x1p)
        res1 = run_bass_kernel_spmd(nc1, maps1, core_ids=list(range(8)))
        res = res1
    else:
        nonces = np.random.randint(1, 2 ** 30, size=4).astype(np.int32)
        nc = build_program(with_l0=True, with_l1=True)
        maps0 = _prep_l0(inputs, nonces)
        maps1 = _prep_l1(inputs, nonces)
        maps = [dict(m0, **m1) for m0, m1 in zip(maps0, maps1)]
        res = run_bass_kernel_spmd(nc, maps, core_ids=list(range(8)))
    out = np.empty((B, T, D), np.float32)
    for c in range(8):
        bb, h = divmod(c, 2)
        out[bb, h * TOK:(h + 1) * TOK, :] = unpack_x_shard(res.results[c]["yout"])
    return out


# version 07
# speedup vs baseline: 1.9208x; 1.9208x over previous
import sys
if '/opt/trn_rl_repo' not in sys.path:
    sys.path.insert(0, '/opt/trn_rl_repo')
import numpy as np
import ml_dtypes
from contextlib import ExitStack

import concourse.bass as bass
import concourse.mybir as mybir
from concourse.bass_utils import run_bass_kernel_spmd
from concourse.expressions_rust import RuntimeValue

F32 = mybir.dt.float32
BF16 = mybir.dt.bfloat16
I32 = mybir.dt.int32
AF = mybir.ActivationFunctionType
ALU = mybir.AluOpType

P = 128
B, T, D = 4, 2048, 2048
TOK = 1024
NT = 512
DC = D // P
H0, H0C = 5632, 44
H1, H1C = 8192, 64
XNSTR = TOK + 1
LN_EPS = 1e-5
QEPS = 1e-8

ENGINES = ("sync", "scalar", "vector", "tensor", "gpsimd")


def _f32(a):
    return np.ascontiguousarray(a, dtype=np.float32)


def pack_panels(WT):
    din, dout = WT.shape
    KC, MC = din // P, dout // P
    t = _f32(WT).reshape(KC, P, MC, P).transpose(2, 1, 0, 3)
    return _f32(t.reshape(MC, P, KC * P))


def pack_x_shard(xbt):
    tc = xbt.shape[0]
    t = _f32(xbt).T.reshape(DC, P, tc).transpose(1, 0, 2)
    return _f32(t.reshape(P, DC * tc))


def unpack_x_shard(arr, tc=TOK):
    t = np.asarray(arr).reshape(P, DC, tc).transpose(1, 0, 2)
    return t.reshape(D, tc).T


def pack_vec(v):
    return _f32(np.asarray(v).reshape(-1, P).T)


def hadamard128_bf16():
    h = np.array([[1.0]], dtype=np.float32)
    while h.shape[0] < P:
        h = np.block([[h, h], [h, -h]])
    return h.astype(ml_dtypes.bfloat16)



class KB:
    def __init__(self, nc, blk, sems):
        self.nc, self.blk, self.sems = nc, blk, sems
        self.cnt = {k: 0 for k in sems}
        self.xw_role = {"w": 0, "r": 0}
        self.waited = {e: {} for e in ENGINES}
        self.bank_free = [None] * 8
        self.pb = None
        self.stream_guard = (None, None)

    def body(self, eng, fn):
        getattr(self.blk, eng)(fn)

    def bump(self, s, n):
        self.cnt[s] += n
        return self.cnt[s]

    def mark(self, s):
        return (s, self.cnt[s])

    def W(self, E, eng, mk):
        if mk is None:
            return
        s, v = mk
        if v <= 0 or self.waited[eng].get(s, 0) >= v:
            return
        self.waited[eng][s] = v
        E.wait_ge(self.sems[s], v)

    def dma(self, E, eng, q, out, in_, waits=()):
        for m in waits:
            self.W(E, eng, m)
        v = self.bump(q, 16)
        E.dma_start(out=out, in_=in_).then_inc(self.sems[q], 16)
        return (q, v)

    def fin(self, inst, s):
        v = self.bump(s, 1)
        inst.then_inc(self.sems[s], 1)
        return (s, v)

    def bank(self, E, eng, i):
        self.W(E, eng, self.bank_free[i])
        return self.pb[i]


class Rot:

    def __init__(self, kb, base):
        self.kb = kb
        self.names = [f"{base}_{i}" for i in range(3)]
        self.n = 0

    def issue(self, E, out, in_):
        nm = self.names[self.n % 3]
        E.dma_start(out=out, in_=in_).then_inc(self.kb.sems[nm], 16)
        self.kb.bump(nm, 16)
        self.n += 1
        return self.n - 1

    def mark(self, i):
        return (self.names[i % 3], 16 * (i // 3 + 1))


def build_program(with_l0=True, with_l1=True, debug=False):
    nc = bass.Bass("TRN2", target_bir_lowering=False)

    d = {}
    def EIN(name, shape, dt=F32):
        d[name] = nc.dram_tensor(name, shape, dt, kind="ExternalInput").ap()

    EIN("ctl", [1, 12], I32)
    if with_l0:
        EIN("xin", [P, DC * TOK])
        EIN("w0p", [DC, P, D])
        EIN("w1p", [H0C, P, D])
        EIN("w2p", [H0C, P, D])
        EIN("w3p", [DC, P, H0])
        EIN("b0t", [P, DC])
    if with_l1:
        if not with_l0:
            EIN("x1in", [P, DC * TOK])
        for nm in ("wkp", "wvp", "wrp", "wop"):
            EIN(nm, [DC, P, D])
        for nm in ("bkt", "bvt", "brt", "bot", "tmkt", "tmvt", "tmrt", "ut", "wt",
                   "sugt", "wsdpt"):
            EIN(nm, [P, DC])
        for nm in ("sdpt", "sut", "sgt"):
            EIN(nm, [P, H1C])
        EIN("wup", [H1C, P, D])
        EIN("wgp", [H1C, P, D])
        EIN("wdp", [DC, P, H1])
        EIN("h128", [P, P], BF16)

    dbg = {}
    if debug and with_l0:
        for nm, cols, dt_ in (("dbg_xn", DC * XNSTR, BF16), ("dbg_lam", DC * TOK, F32),
                              ("dbg_s0", DC * TOK, BF16), ("dbg_h0", H0C * TOK, BF16),
                              ("dbg_tq", 16, F32), ("dbg_sin", 2 * DC, F32), ("dbg_gj", 40, F32), ("dbg_rows", 16, F32)):
            dbg[nm] = nc.dram_tensor(nm, [P, cols], dt_, kind="ExternalOutput").ap()
    if debug and with_l1:
        for nm, cols, dt_ in (("dbg1_xn", DC * XNSTR, BF16), ("dbg1_k", DC * TOK, BF16),
                              ("dbg1_v", DC * TOK, BF16), ("dbg1_r", DC * TOK, BF16),
                              ("dbg1_ab", 4 * DC, F32), ("dbg1_wkvr", DC * TOK, BF16),
                              ("dbg1_z", DC * TOK, BF16), ("dbg1_z2", DC * TOK, BF16), ("dbg1_sb", DC * NT, BF16), ("dbg1_h", H1C * TOK, BF16),
                              ("dbg1_hh", H1C * TOK, BF16), ("dbg1_s1", DC * TOK, BF16)):
            dbg[nm] = nc.dram_tensor(nm, [P, cols], dt_, kind="ExternalOutput").ap()
    if with_l0 and not with_l1:
        x1buf = nc.dram_tensor("x1out", [P, DC * TOK], F32, kind="ExternalOutput").ap()
    elif with_l0:
        x1buf = nc.dram_tensor("x1mid", [P, DC * TOK], F32).ap()
    else:
        x1buf = d["x1in"]
    yout = s1sp = None
    if with_l1:
        yout = nc.dram_tensor("yout", [P, DC * TOK], F32, kind="ExternalOutput").ap()
        s1sp = nc.dram_tensor("s1spill", [P, DC * TOK], BF16).ap()

    shr = {}
    def SHR(name, shape, dt=F32):
        shr[name] = nc.dram_tensor(name, shape, dt, addr_space="Shared").ap()
    if with_l0:
        SHR("pay_g0", [P, 4]); SHR("pay_g1", [P, 4])
        SHR("flag_g0", [1, 16], I32); SHR("flag_g1", [1, 16], I32)
        SHR("pay_s", [P, DC]); SHR("flag_s", [1, 16], I32)
    if with_l1:
        SHR("pay_xn", [P, DC], BF16); SHR("flag_xn", [1, 16], I32)
        SHR("pay_ab", [P, 2 * DC]); SHR("flag_ab", [1, 16], I32)

    SEMNAMES = ["qS", "qA", "qG", "eT", "eS", "eV", "eP", "xd",
                "xw", "xp_g", "xp_g2", "xp_s", "xp_xn", "xp_ab",
                "qS_0", "qS_1", "qS_2", "qG_0", "qG_1", "qG_2",
                "qX_0", "qX_1", "qX_2"]

    with ExitStack() as ST:
        E_ = ST.enter_context
        ARENA_COLS = 49168
        arena_t = E_(nc.sbuf_tensor("arena", [P, ARENA_COLS], F32))
        arena_f = arena_t.ap()
        arena_b = arena_f.bitcast(BF16)

        def AFP(off, cols):
            assert off % 16 == 0 and off + cols * 4 <= ARENA_COLS * 4, (off, cols)
            return arena_f[:, off // 4: off // 4 + cols]

        def ABF(off, cols):
            assert off % 16 == 0 and off + cols * 2 <= ARENA_COLS * 4, (off, cols)
            return arena_b[:, off // 2: off // 2 + cols]

        ctl_sb = E_(nc.sbuf_tensor("ctl_sb", [1, 12], I32))
        fsb = E_(nc.sbuf_tensor("fsb", [1, 16], I32))
        flgsrc = E_(nc.sbuf_tensor("flgsrc", [1, 16], I32))
        pcnt = E_(nc.sbuf_tensor("pcnt", [1, 8], I32))
        ones_b = E_(nc.sbuf_tensor("ones_b", [P, 1], BF16))
        onesf_col = E_(nc.sbuf_tensor("onesf_col", [P, 1], F32))
        onesf_row = E_(nc.sbuf_tensor("onesf_row", [1, P], F32))
        smalls = E_(nc.sbuf_tensor("smalls", [P, 512], F32))
        rows = E_(nc.sbuf_tensor("rows", [1, 4 * NT], F32))
        grow_t = E_(nc.sbuf_tensor("grow_t", [1, 4], F32))
        trow_t = E_(nc.sbuf_tensor("trow_t", [1, 4], F32))
        if with_l1:
            sdp_sb = E_(nc.sbuf_tensor("sdp_sb", [P, H1C], F32))
            su2_sb = E_(nc.sbuf_tensor("su2_sb", [P, H1C], F32))
            sg_sb = E_(nc.sbuf_tensor("sg_sb", [P, H1C], F32))
            h128_sb = E_(nc.sbuf_tensor("h128_sb", [P, P], BF16))
        psb = [E_(nc.psum_tensor(f"pb{i}", [P, NT], F32)) for i in range(8)]
        sems = {n: E_(nc.semaphore(n)) for n in SEMNAMES}
        blk = E_(nc.Block())

        kb = KB(nc, blk, sems)
        kb.pb = [t.ap() for t in psb]
        kb.rS = Rot(kb, "qS")
        kb.rG = Rot(kb, "qG")
        kb.rX = Rot(kb, "qX")
        body = kb.body

        SM = {"_off": 0}
        def sm(n):
            off = SM["_off"]; SM["_off"] = off + n
            assert SM["_off"] <= 512
            return smalls.ap()[:, off:off + n]
        b0sb = sm(DC)
        accg, accg2 = sm(4), sm(4)
        Tq, nTq = sm(4), sm(4)
        sfin, s_in = sm(DC), sm(DC)
        redtmp = sm(4)
        gpart = sm(4)
        gjd = sm(40)
        if with_l1:
            bk_sb, bv_sb, br_sb, bo_sb = sm(DC), sm(DC), sm(DC), sm(DC)
            tmk_sb, tmv_sb, tmr_sb = sm(DC), sm(DC), sm(DC)
            u_sb, w_sb, nw_sb, uw_sb = sm(DC), sm(DC), sm(DC), sm(DC)
            sug_sb, wsdp_sb = sm(DC), sm(DC)
            afin, bfin, a_in, b_in = sm(DC), sm(DC), sm(DC), sm(DC)

        ssum = rows.ap()[0:1, 0:NT]
        sqsum = rows.ap()[0:1, NT:2 * NT]
        mrow = rows.ap()[0:1, 2 * NT:3 * NT]
        rrow = rows.ap()[0:1, 3 * NT:4 * NT]

        def XN(c, lo, n):
            start = c * XNSTR + 1 + lo
            return arena_b[:, start:start + n]

        _rid = [0]

        def rid(base):
            _rid[0] += 1
            return f"{base}{_rid[0]}"

        sq_strips = [ABF(188480, NT), ABF(188480 + 1024, NT)]
        xb_strips = [ABF(188480 + 2048, NT), ABF(188480 + 3072, NT)]
        lntmp = AFP(192576, NT)

        def _poll(E, flag_dram, fexp_reg, xq, slot):
            sem = sems[xq]
            with E.register(rid("rc")) as rcnt, E.register(rid("fc")) as fcur:
                E.reg_mov(rcnt, 0)
                E.dma_start(out=fsb[:], in_=flag_dram).then_inc(sem, 16)
                E.reg_add(rcnt, rcnt, 16)
                E.wait_ge(sem, rcnt)
                E.load(fcur, fsb.ap()[0:1, 0:1])

                def cond():
                    return RuntimeValue(fcur) - RuntimeValue(fexp_reg)
                with E.While(cond):
                    E.dma_start(out=fsb[:], in_=flag_dram).then_inc(sem, 16)
                    E.reg_add(rcnt, rcnt, 16)
                    E.wait_ge(sem, rcnt)
                    E.load(fcur, fsb.ap()[0:1, 0:1])
                E.reg_save(pcnt.ap()[0:1, slot:slot + 1], rcnt)

        def _read_payloads(E, xq, slot, pairs):
            for dst, src in pairs:
                E.dma_start(out=dst, in_=src).then_inc(sems[xq], 16)
            with E.register(rid("rw")) as r:
                E.load(r, pcnt.ap()[0:1, slot:slot + 1])
                E.reg_add(r, r, 16 * len(pairs))
                E.wait_ge(sems[xq], r)

        def _write_payloads(E, role, pairs, flag_dram, nonce_col):
            base = kb.xw_role[role]
            for dst, src in pairs:
                E.dma_start(out=dst, in_=src).then_inc(sems["xw"], 16)
            kb.xw_role[role] += 16 * len(pairs)
            E.wait_ge(sems["xw"], kb.xw_role[role])
            with E.register(rid("rn")) as rn:
                E.load(rn, ctl_sb.ap()[0:1, nonce_col:nonce_col + 1])
                E.store(flgsrc.ap()[0:1, 0:1], rn)
            E.dma_start(out=flag_dram, in_=flgsrc[:]).then_inc(sems["xw"], 16)
            kb.xw_role[role] += 16

        XSLOT = {"xp_g": 0, "xp_g2": 4, "xp_s": 1, "xp_xn": 2, "xp_ab": 3}

        def emit_exchange(site, nonce_col, pay_w, pay_r, ready_mk, extra_mks=(),
                          w_role="w", r_is="is_r"):
            xq = f"xp_{site}"
            fname = {"g": "flag_g0", "g2": "flag_g1"}.get(site, f"flag_{site}")
            flag = shr[fname]

            def f_gp(E):
                kb.W(E, "gpsimd", ready_mk)
                for m in extra_mks:
                    kb.W(E, "gpsimd", m)
                with E.register(rid("isw")) as is_w, E.register(rid("isr")) as is_r, \
                     E.register(rid("fxp")) as fexp:
                    wcol = 0 if w_role == "w" else 1
                    E.load(is_w, ctl_sb.ap()[0:1, wcol:wcol + 1])
                    E.load(is_r, ctl_sb.ap()[0:1, (1 - wcol):(2 - wcol)])
                    E.load(fexp, ctl_sb.ap()[0:1, nonce_col:nonce_col + 1])
                    with E.If(is_w):
                        _write_payloads(E, w_role, pay_w, flag, nonce_col)
                    E.end_ifs()
                    with E.If(is_r):
                        _poll(E, flag, fexp, xq, XSLOT[xq])
                        _read_payloads(E, xq, XSLOT[xq], pay_r)
                    E.end_ifs()
                E.sem_inc(sems["xd"], 1)
            body("gpsimd", f_gp)
            return ("xd", kb.bump("xd", 1))

        STR_OFF = 163904

        def stream_panels(jobs, kchunks, kgroups, rhs_fn, evac_fn, quant_col=None,
                          cast_pre=(), banks=(0, 1, 2, 3), qtmp_off=188480,
                          str_off=None, str_cap=188480, dma_pre=(), cast_eng="scalar",
                          pool_cols=None):
            if str_off is None:
                str_off = STR_OFF
            n_j = len(jobs)
            kcg = kchunks // kgroups
            fcols = kcg * P
            fsz16 = ((fcols * 4 + 15) // 16) * 16
            bsz16 = ((fcols * 2 + 15) // 16) * 16
            ftiles = [AFP(str_off, fcols), AFP(str_off + fsz16, fcols)]
            boff = str_off + 2 * fsz16
            btiles = [ABF(boff, fcols), ABF(boff + bsz16, fcols)]
            assert boff + 2 * bsz16 <= str_cap
            qtmp = ABF(qtmp_off, fcols)
            n_it = n_j * kgroups
            qc = None
            if quant_col is not None and quant_col != "pool_cmp":
                qc = quant_col if callable(quant_col) else (lambda it: quant_col)

            prev_cast, prev_mm = kb.stream_guard
            cast_mks = [None] * n_it
            mm_mks = {}

            def bank_of(j, h):
                return banks[(2 * j + h) % len(banks)]

            for it in range(n_it):
                j, g = divmod(it, kgroups)

                def f_dma(E, it=it, j=j, g=g):
                    if it == 0 and prev_cast is not None:
                        kb.W(E, "sync", prev_cast)
                    if it == 0:
                        for m_ in dma_pre:
                            kb.W(E, "sync", m_)
                    if it >= 2:
                        kb.W(E, "sync", cast_mks[it - 2])
                    kb.rS.issue(E, ftiles[it % 2][:, 0:fcols],
                                jobs[j][:, g * fcols:(g + 1) * fcols])
                body("sync", f_dma)
                dma_mk = kb.rS.mark(kb.rS.n - 1)

                if quant_col == "pool_cmp":
                    pool_col = pool_cols if callable(pool_cols) else (lambda it: 0)
                    getmp = ABF(qtmp_off, fcols)
                    letmp = ABF(qtmp_off + ((fcols * 2 + 15) // 16) * 16, fcols)

                    def f_cmp(E, it=it, dma_mk=dma_mk):
                        for m in cast_pre:
                            kb.W(E, "gpsimd", m)
                        if it == 0 and prev_mm is not None:
                            kb.W(E, "gpsimd", prev_mm)
                        kb.W(E, "gpsimd", dma_mk)
                        if it >= 1:
                            kb.W(E, "gpsimd", cast_mks[it - 1])
                        col = pool_col(it)
                        E.tensor_scalar(out=getmp, in0=ftiles[it % 2][:, 0:fcols],
                                        scalar1=Tq[:, col:col + 1], scalar2=None,
                                        op0=ALU.is_ge)
                        kb.fin(E.tensor_scalar(out=letmp, in0=ftiles[it % 2][:, 0:fcols],
                                               scalar1=nTq[:, col:col + 1], scalar2=None,
                                               op0=ALU.is_le), "eP")
                    body("gpsimd", f_cmp)
                    cmp_mk = kb.mark("eP")

                    def f_sub(E, it=it, cmp_mk=cmp_mk):
                        kb.W(E, "vector", cmp_mk)
                        if it >= 2:
                            kb.W(E, "vector", mm_mks[(it - 2, 1)])
                        kb.fin(E.tensor_tensor(out=btiles[it % 2][:, 0:fcols],
                                               in0=getmp, in1=letmp,
                                               op=ALU.subtract), "eV")
                    body("vector", f_sub)
                    cast_mks[it] = kb.mark("eV")
                    ftfree_mks = None
                elif qc is not None:
                    def f_sgn(E, it=it, dma_mk=dma_mk):
                        for m in cast_pre:
                            kb.W(E, "scalar", m)
                        if it == 0 and prev_mm is not None:
                            kb.W(E, "scalar", prev_mm)
                        kb.W(E, "scalar", dma_mk)
                        if it >= 2:
                            kb.W(E, "scalar", mm_mks[(it - 2, 1)])
                        if it >= 1:
                            kb.W(E, "scalar", cast_mks[it - 1])
                        col = qc(it)
                        E.sign(out=btiles[it % 2][:, 0:fcols],
                               in_=ftiles[it % 2][:, 0:fcols], bias=nTq[:, col:col + 1])
                        kb.fin(E.sign(out=qtmp, in_=ftiles[it % 2][:, 0:fcols],
                                      bias=Tq[:, col:col + 1]), "eS")
                    body("scalar", f_sgn)
                    sgn_mk = kb.mark("eS")

                    def f_add(E, it=it, sgn_mk=sgn_mk):
                        kb.W(E, "vector", sgn_mk)
                        kb.fin(E.tensor_tensor(out=btiles[it % 2][:, 0:fcols],
                                               in0=btiles[it % 2][:, 0:fcols], in1=qtmp,
                                               op=ALU.add), "eV")
                    body("vector", f_add)
                    cast_mks[it] = kb.mark("eV")
                elif cast_eng == "gpsimd":
                    def f_cst(E, it=it, dma_mk=dma_mk):
                        for m in cast_pre:
                            kb.W(E, "gpsimd", m)
                        if it == 0 and prev_mm is not None:
                            kb.W(E, "gpsimd", prev_mm)
                        kb.W(E, "gpsimd", dma_mk)
                        if it >= 2:
                            kb.W(E, "gpsimd", mm_mks[(it - 2, 1)])
                        kb.fin(E.tensor_copy(out=btiles[it % 2][:, 0:fcols],
                                             in_=ftiles[it % 2][:, 0:fcols]), "eP")
                    body("gpsimd", f_cst)
                    cast_mks[it] = kb.mark("eP")
                else:
                    def f_cst(E, it=it, dma_mk=dma_mk):
                        for m in cast_pre:
                            kb.W(E, "scalar", m)
                        if it == 0 and prev_mm is not None:
                            kb.W(E, "scalar", prev_mm)
                        kb.W(E, "scalar", dma_mk)
                        if it >= 2:
                            kb.W(E, "scalar", mm_mks[(it - 2, 1)])
                        kb.fin(E.copy(out=btiles[it % 2][:, 0:fcols],
                                      in_=ftiles[it % 2][:, 0:fcols]), "eS")
                    body("scalar", f_cst)
                    cast_mks[it] = kb.mark("eS")

                def f_mm(E, it=it, j=j, g=g):
                    kb.W(E, "tensor", cast_mks[it])
                    for h in range(2):
                        bank = kb.bank(E, "tensor", bank_of(j, h))
                        inst = None
                        for k in range(kcg):
                            inst = E.matmul(bank, btiles[it % 2][:, k * P:(k + 1) * P],
                                            rhs_fn(g * kcg + k, h),
                                            start=(g == 0 and k == 0),
                                            stop=(g == kgroups - 1 and k == kcg - 1))
                        kb.fin(inst, "eT")
                body("tensor", f_mm)
                for h in range(2):
                    mm_mks[(it, h)] = ("eT", kb.cnt["eT"] - 1 + h)

                if g == kgroups - 1:
                    for h in range(2):
                        rd = evac_fn(j, h, bank_of(j, h), mm_mks[(it, h)])
                        kb.bank_free[bank_of(j, h)] = rd

            kb.stream_guard = (cast_mks[-1], mm_mks[(n_it - 1, 1)])

        if with_l0:
            def f_c1(E):
                kb.dma(E, "scalar", "qA", ctl_sb[:], d["ctl"])
                kb.dma(E, "scalar", "qA", b0sb, d["b0t"])
            body("scalar", f_c1)
            ctl_mk = kb.mark("qA")

            def f_c2(E):
                E.memset(ones_b[:], 1.0)
                E.memset(onesf_col[:], 1.0)
                E.memset(onesf_row[:], 1.0)
                E.memset(s_in, 0.0)
                E.memset(accg, 0.0)
                E.memset(accg2, 0.0)
                kb.fin(E.memset(flgsrc[:], 0), "eV")
            body("vector", f_c2)
            consts_mk = kb.mark("eV")

            XOFF = 32832
            Xf = AFP(XOFF, DC * TOK)

            def xch(c, h=None):
                if h is None:
                    return Xf[:, c * TOK:(c + 1) * TOK]
                return Xf[:, c * TOK + h * NT:c * TOK + (h + 1) * NT]

            def f_xl(E):
                kb.dma(E, "scalar", "qA", Xf, d["xin"])
            body("scalar", f_xl)
            x_mk = kb.mark("qA")

            xn_done = {}
            for h in range(2):
                eS0 = kb.cnt["eS"]; eV0 = kb.cnt["eV"]; eT0 = kb.cnt["eT"]

                def f_sq(E, h=h, eT0=eT0):
                    kb.W(E, "scalar", x_mk)
                    for c in range(DC):
                        if c >= 2:
                            E.wait_ge(sems["eT"], eT0 + c - 1)
                        E.square(out=sq_strips[c % 2], in_=xch(c, h)).then_inc(sems["eS"], 1)
                body("scalar", f_sq)
                kb.bump("eS", DC)

                def f_cp(E, h=h, eT0=eT0):
                    kb.W(E, "vector", x_mk)
                    kb.W(E, "vector", consts_mk)
                    for c in range(DC):
                        if c >= 2:
                            E.wait_ge(sems["eT"], eT0 + c - 1)
                        E.tensor_copy(out=xb_strips[c % 2], in_=xch(c, h)).then_inc(sems["eV"], 1)
                body("vector", f_cp)
                kb.bump("eV", DC)

                bA, bB = (0, 1) if h == 0 else (2, 3)

                def f_st(E, h=h, bA=bA, bB=bB, eS0=eS0, eV0=eV0):
                    bankA = kb.bank(E, "tensor", bA)
                    bankB = kb.bank(E, "tensor", bB)
                    for c in range(DC):
                        E.wait_ge(sems["eS"], eS0 + c + 1)
                        E.wait_ge(sems["eV"], eV0 + c + 1)
                        E.matmul(bankA[0:1, :], ones_b[:], xb_strips[c % 2],
                                 start=(c == 0), stop=(c == DC - 1))
                        E.matmul(bankB[0:1, :], ones_b[:], sq_strips[c % 2],
                                 start=(c == 0), stop=(c == DC - 1)).then_inc(sems["eT"], 1)
                        kb.bump("eT", 1)
                body("tensor", f_st)
                smm_mk = kb.mark("eT")

                def f_sf(E, bA=bA, bB=bB, smm_mk=smm_mk):
                    kb.W(E, "vector", smm_mk)
                    E.tensor_scalar(out=mrow, in0=kb.pb[bA][0:1, :], scalar1=1.0 / D,
                                    scalar2=None, op0=ALU.mult)
                    E.tensor_scalar(out=ssum, in0=kb.pb[bB][0:1, :], scalar1=1.0 / D,
                                    scalar2=None, op0=ALU.mult)
                    E.tensor_tensor(out=sqsum, in0=mrow, in1=mrow, op=ALU.mult)
                    E.tensor_tensor(out=ssum, in0=ssum, in1=sqsum, op=ALU.subtract)
                    kb.fin(E.tensor_scalar(out=ssum, in0=ssum, scalar1=LN_EPS,
                                           scalar2=None, op0=ALU.add), "eV")
                body("vector", f_sf)
                var_mk = kb.mark("eV")
                kb.bank_free[bA] = var_mk
                kb.bank_free[bB] = var_mk

                def f_sqr(E, var_mk=var_mk):
                    kb.W(E, "scalar", var_mk)
                    kb.fin(E.sqrt(out=sqsum, in_=ssum), "eS")
                body("scalar", f_sqr)
                sd_mk = kb.mark("eS")

                def f_rc(E, sd_mk=sd_mk):
                    kb.W(E, "vector", sd_mk)
                    kb.fin(E.reciprocal(out=rrow, in_=sqsum), "eV")
                body("vector", f_rc)
                rstd_mk = kb.mark("eV")

                bM, bR = (4, 5) if h == 0 else (6, 7)

                def f_bc(E, bM=bM, bR=bR, rstd_mk=rstd_mk):
                    kb.W(E, "tensor", rstd_mk)
                    bankM = kb.bank(E, "tensor", bM)
                    bankR = kb.bank(E, "tensor", bR)
                    E.matmul(bankM, onesf_row[:], mrow, start=True, stop=True)
                    kb.fin(E.matmul(bankR, onesf_row[:], rrow, start=True, stop=True), "eT")
                body("tensor", f_bc)
                bc_mk = kb.mark("eT")

                def f_xn(E, h=h, bM=bM, bR=bR, bc_mk=bc_mk):
                    kb.W(E, "vector", bc_mk)
                    for c in range(DC):
                        E.tensor_tensor(out=lntmp, in0=xch(c, h), in1=kb.pb[bM],
                                        op=ALU.subtract)
                        E.tensor_tensor(out=XN(c, h * NT, NT), in0=lntmp, in1=kb.pb[bR],
                                        op=ALU.mult)
                    kb.fin(E.engine_nop(), "eV")
                body("vector", f_xn)
                xn_done[h] = kb.mark("eV")
                kb.bank_free[bM] = xn_done[h]
                kb.bank_free[bR] = xn_done[h]

            xn_mk = xn_done[1]

            gtiles = [AFP(98368, 4096), AFP(114752, 4096),
                      AFP(131136, 4096), AFP(147520, 4096)]
            w1v = d["w1p"].rearrange("m p c -> p m c")
            w2v = d["w2p"].rearrange("m p c -> p m c")
            w3v = d["w3p"].rearrange("m p c -> p m c")
            gjobs = []
            for i in range(11):
                gjobs.append((0, w1v[:, 2 * i:2 * i + 2, :], w1v[:, 22 + 2 * i:24 + 2 * i, :], 4096))
            for i in range(11):
                gjobs.append((1, w2v[:, 2 * i:2 * i + 2, :], w2v[:, 22 + 2 * i:24 + 2 * i, :], 4096))
            for i in range(16):
                m0, k0 = i // 2, (i % 2) * 2816
                gjobs.append((2, w3v[:, m0:m0 + 1, k0:k0 + 2816],
                              w3v[:, 8 + m0:9 + m0, k0:k0 + 2816], 2816))
            nred = len(gjobs)
            eV0g = kb.cnt["eV"]
            grot0 = kb.rG.n

            def f_gd(E):
                kb.W(E, "gpsimd", ctl_mk)
                with E.register(rid("gisw")) as is_w:
                    E.load(is_w, ctl_sb.ap()[0:1, 0:1])
                    with E.If(is_w):
                        for i, (j, va, vb, n) in enumerate(gjobs):
                            if i >= 4:
                                E.wait_ge(sems["eV"], eV0g + i - 3)
                            nm = kb.rG.names[(grot0 + i) % 3]
                            E.dma_start(out=gtiles[i % 4][:, 0:n], in_=va).then_inc(sems[nm], 16)
                    E.end_ifs()
                    with E.If_eq(is_w, 0):
                        for i, (j, va, vb, n) in enumerate(gjobs):
                            if i >= 4:
                                E.wait_ge(sems["eV"], eV0g + i - 3)
                            nm = kb.rG.names[(grot0 + i) % 3]
                            E.dma_start(out=gtiles[i % 4][:, 0:n], in_=vb).then_inc(sems[nm], 16)
                    E.end_ifs()
            body("gpsimd", f_gd)
            for i in range(nred):
                kb.bump(kb.rG.names[(grot0 + i) % 3], 16)
                kb.rG.n += 1
            gdma_mk = kb.rG.mark(kb.rG.n - 1)

            def f_gr(E):
                kb.W(E, "vector", consts_mk)
                def add_of(i):
                    j = gjobs[i][0]
                    E.tensor_tensor(out=accg[:, j:j + 1], in0=accg[:, j:j + 1],
                                    in1=redtmp[:, (i % 2):(i % 2) + 1],
                                    op=ALU.add).then_inc(sems["eV"], 1)
                for i, (j, va, vb, n) in enumerate(gjobs):
                    gi = grot0 + i
                    E.wait_ge(sems[kb.rG.names[gi % 3]], 16 * (gi // 3 + 1))
                    E.tensor_reduce(out=redtmp[:, (i % 2):(i % 2) + 1],
                                    in_=gtiles[i % 4][:, 0:n],
                                    axis=mybir.AxisListType.X, op=ALU.add,
                                    apply_absolute_value=True)
                    if i >= 1:
                        add_of(i - 1)
                E.tensor_copy(out=gtiles[0][:, 0:256], in_=gtiles[0][:, 256:512])
                add_of(nred - 1)
            body("vector", f_gr)
            kb.bump("eV", nred)

            def f_grc(E):
                kb.fin(E.engine_nop(), "eV")
            body("vector", f_grc)
            gred_mk = kb.mark("eV")

            xd_g = emit_exchange("g", 2, [(shr["pay_g0"], accg)],
                                 [(accg2, shr["pay_g0"])], ready_mk=gred_mk,
                                 extra_mks=(consts_mk,), w_role="w")
            xd_g2 = emit_exchange("g2", 2, [(shr["pay_g1"], accg)],
                                  [(accg2, shr["pay_g1"])], ready_mk=gred_mk,
                                  w_role="r")



            LOFF = 32832
            LAM = AFP(LOFF, DC * TOK)

            def lam_ch(c, h=None):
                if h is None:
                    return LAM[:, c * TOK:(c + 1) * TOK]
                return LAM[:, c * TOK + h * NT:c * TOK + (h + 1) * NT]

            lam_evac_mks = {}

            def lam_evac(m, h, bidx, mm_mk):
                def f(E, m=m, h=h, bidx=bidx, mm_mk=mm_mk):
                    kb.W(E, "scalar", mm_mk)
                    kb.fin(E.activation(out=lam_ch(m, h), in_=kb.pb[bidx],
                                        func=AF.Sigmoid, bias=b0sb[:, m:m + 1]), "eS")
                body("scalar", f)
                lam_evac_mks[(m, h)] = kb.mark("eS")
                return lam_evac_mks[(m, h)]

            stream_panels([d["w0p"][m] for m in range(DC)], DC, 1,
                          rhs_fn=lambda k, h: XN(k, h * NT, NT),
                          evac_fn=lam_evac, cast_pre=(xn_mk,))

            def f_gf(E):
                kb.W(E, "vector", xd_g)
                kb.W(E, "vector", xd_g2)
                E.tensor_copy(out=gpart, in_=accg)
                kb.fin(E.tensor_tensor(out=accg, in0=accg, in1=accg2, op=ALU.add), "eV")
            body("vector", f_gf)
            gtot_mk = kb.mark("eV")

            def f_gt(E):
                kb.W(E, "tensor", gtot_mk)
                bank = kb.bank(E, "tensor", 6)
                kb.fin(E.matmul(bank[0:1, 0:4], onesf_col[:], accg, start=True,
                                stop=True), "eT")
            body("tensor", f_gt)
            gmm_mk = kb.mark("eT")

            def f_gs(E, gmm_mk=gmm_mk):
                kb.W(E, "vector", gmm_mk)
                E.tensor_scalar(out=grow_t[:], in0=kb.pb[6][0:1, 0:4],
                                scalar1=1.0 / (H0 * D), scalar2=None, op0=ALU.mult)
                kb.fin(E.tensor_scalar(out=trow_t[:], in0=kb.pb[6][0:1, 0:4],
                                       scalar1=0.5 / (H0 * D),
                                       scalar2=0.5 * QEPS, op0=ALU.mult, op1=ALU.add), "eV")
            body("vector", f_gs)
            kb.bank_free[6] = kb.mark("eV")
            trow_mk = kb.mark("eV")

            def f_gt2(E):
                kb.W(E, "tensor", trow_mk)
                bank = kb.bank(E, "tensor", 7)
                kb.fin(E.matmul(bank[:, 0:4], onesf_row[:], trow_t[:], start=True,
                                stop=True), "eT")
            body("tensor", f_gt2)
            tb_mk = kb.mark("eT")

            def f_gv(E, tb_mk=tb_mk):
                kb.W(E, "vector", tb_mk)
                E.tensor_copy(out=Tq, in_=kb.pb[7][:, 0:4])
                kb.fin(E.tensor_scalar(out=nTq, in0=kb.pb[7][:, 0:4], scalar1=-1.0,
                                       scalar2=None, op0=ALU.mult), "eV")
            body("vector", f_gv)
            kb.bank_free[7] = kb.mark("eV")
            gamma_mk = kb.mark("eV")

            ustrip = AFP(98368, TOK)
            scscr2 = [AFP(102464, TOK), AFP(106560, TOK)]

            def f_p1(E):
                kb.W(E, "vector", gdma_mk)
                for c in range(DC):
                    kb.W(E, "vector", lam_evac_mks[(c, 1)])
                    E.tensor_scalar(out=ustrip, in0=lam_ch(c), scalar1=-1.0, scalar2=1.0,
                                    op0=ALU.mult, op1=ALU.add)
                    E.tensor_tensor(out=ustrip, in0=ustrip, in1=XN(c, 0, TOK), op=ALU.mult)
                    E.tensor_tensor_scan(out=scscr2[c % 2], data0=lam_ch(c), data1=ustrip,
                                         initial=0.0, op0=ALU.mult, op1=ALU.add)
                    if c >= 1:
                        kb.fin(E.tensor_copy(out=sfin[:, c - 1:c],
                                             in_=scscr2[(c - 1) % 2][:, TOK - 1:TOK]), "eV")
                E.tensor_copy(out=ustrip[:, 0:256], in_=ustrip[:, 256:512])
                kb.fin(E.tensor_copy(out=sfin[:, DC - 1:DC],
                                     in_=scscr2[(DC - 1) % 2][:, TOK - 1:TOK]), "eV")
            body("vector", f_p1)
            p1_mk = kb.mark("eV")

            xd_s = emit_exchange("s", 3, [(shr["pay_s"], sfin)], [(s_in, shr["pay_s"])],
                                 ready_mk=p1_mk, extra_mks=(consts_mk,))

            S0OFF = 131136
            S0B = ABF(S0OFF, DC * TOK)

            def s0_ch(c, h=None):
                if h is None:
                    return S0B[:, c * TOK:(c + 1) * TOK]
                return S0B[:, c * TOK + h * NT:c * TOK + (h + 1) * NT]

            def f_p2(E):
                kb.W(E, "vector", xd_s)
                kb.W(E, "vector", gred_mk)
                for c in range(DC):
                    E.tensor_scalar(out=ustrip, in0=lam_ch(c), scalar1=-1.0, scalar2=1.0,
                                    op0=ALU.mult, op1=ALU.add)
                    E.tensor_tensor(out=ustrip, in0=ustrip, in1=XN(c, 0, TOK), op=ALU.mult)
                    kb.fin(E.tensor_tensor_scan(out=s0_ch(c), data0=lam_ch(c),
                                                data1=ustrip, initial=s_in[:, c:c + 1],
                                                op0=ALU.mult, op1=ALU.add), "eV")
            body("vector", f_p2)
            p5_mk = kb.mark("eV")
            if debug:
                def f_dbg1(E):
                    kb.W(E, "gpsimd", p5_mk)
                    kb.W(E, "gpsimd", gamma_mk)
                    kb.dma(E, "gpsimd", "qG", dbg["dbg_lam"], LAM)
                    kb.dma(E, "gpsimd", "qG", dbg["dbg_tq"][:, 0:4], Tq)
                    kb.dma(E, "gpsimd", "qG", dbg["dbg_tq"][:, 4:8], gpart)
                    kb.dma(E, "gpsimd", "qG", dbg["dbg_tq"][:, 8:12], accg2)
                    kb.dma(E, "gpsimd", "qG", dbg["dbg_tq"][:, 12:16], accg)
                    kb.dma(E, "gpsimd", "qG", dbg["dbg_sin"][:, 0:DC], sfin)
                    kb.dma(E, "gpsimd", "qG", dbg["dbg_sin"][:, DC:2 * DC], s_in)
                    kb.dma(E, "gpsimd", "qG", dbg["dbg_gj"], gjd)
                    kb.dma(E, "gpsimd", "qG", dbg["dbg_rows"][0:1, 0:4], grow_t[:])
                    kb.dma(E, "gpsimd", "qG", dbg["dbg_rows"][0:1, 4:8], trow_t[:])
                body("gpsimd", f_dbg1)
                dbg1_mk = kb.mark("qG")
                p5_mk = ("qG", dbg1_mk[1])

            H0OFF = 32832
            H0B = ABF(H0OFF, H0C * TOK)

            def h0_ch(m, h=None):
                if m >= H0C:
                    raise IndexError
                if h is None:
                    return H0B[:, m * TOK:(m + 1) * TOK]
                return H0B[:, m * TOK + h * NT:m * TOK + (h + 1) * NT]

            sgst = [ABF(122944, NT), ABF(122944 + 1024, NT),
                    ABF(122944 + 2048, NT), ABF(122944 + 3072, NT)]

            w12jobs = []
            for m in range(H0C):
                w12jobs.append(d["w1p"][m])
                w12jobs.append(d["w2p"][m])
            sig_mks = {}
            h0_mks = {}

            def w12_evac(j, h, bidx, mm_mk):
                m, which = j // 2, j % 2
                if which == 0:
                    def f(E, m=m, h=h, bidx=bidx, mm_mk=mm_mk):
                        kb.W(E, "scalar", mm_mk)
                        if (m - 2, h) in h0_mks:
                            kb.W(E, "scalar", h0_mks[(m - 2, h)])
                        kb.fin(E.activation(out=sgst[2 * (m % 2) + h], in_=kb.pb[bidx],
                                            func=AF.Sigmoid, scale=0.5), "eS")
                    body("scalar", f)
                    sig_mks[(m, h)] = kb.mark("eS")
                    return sig_mks[(m, h)]
                else:
                    def f(E, m=m, h=h, bidx=bidx, mm_mk=mm_mk):
                        kb.W(E, "vector", mm_mk)
                        kb.W(E, "vector", sig_mks[(m, h)])
                        kb.W(E, "vector", p5_mk)
                        kb.W(E, "vector", gred_mk)
                        kb.fin(E.scalar_tensor_tensor(out=h0_ch(m, h), in0=kb.pb[bidx],
                                                      scalar=0.5, in1=sgst[2 * (m % 2) + h],
                                                      op0=ALU.mult, op1=ALU.mult), "eV")
                    body("vector", f)
                    h0_mks[(m, h)] = kb.mark("eV")
                    return h0_mks[(m, h)]

            stream_panels(w12jobs, DC, 1, rhs_fn=lambda k, h: XN(k, h * NT, NT),
                          evac_fn=w12_evac, quant_col=lambda it: it % 2,
                          cast_pre=(gamma_mk,))
            w12_mm_mk = kb.mark("eT")
            if debug:
                def f_dbg2(E):
                    kb.W(E, "gpsimd", w12_mm_mk)
                    kb.dma(E, "gpsimd", "qG", dbg["dbg_xn"], ABF(0, DC * XNSTR))
                body("gpsimd", f_dbg2)
                w12_mm_mk = ("qG", kb.mark("qG")[1])

            gst = [ABF(0, NT), ABF(1024, NT)]
            xrl = [AFP(2048, NT), AFP(2048 + 2048, NT)]
            t1s = AFP(6144, NT)
            onw = [AFP(8192, NT), AFP(8192 + 2048, NT)]
            cmb_mks = {}
            xrl_dma = {}

            def f_seed(E):
                kb.W(E, "gpsimd", w12_mm_mk)
                for i in range(2):
                    m, h = divmod(i, 2)
                    xrl_dma[i] = kb.rX.issue(
                        E, xrl[i % 2],
                        d["xin"][:, m * TOK + h * NT:m * TOK + (h + 1) * NT])
            body("gpsimd", f_seed)

            out_mks = {}

            def w3_evac(m, h, bidx, mm_mk):
                i = 2 * m + h

                def f_g(E, m=m, h=h, bidx=bidx, mm_mk=mm_mk, i=i):
                    kb.W(E, "scalar", mm_mk)
                    if i - 2 in cmb_mks:
                        kb.W(E, "scalar", cmb_mks[i - 2])
                    kb.fin(E.activation(out=gst[h], in_=kb.pb[bidx], func=AF.Silu,
                                        scale=0.5), "eS")
                body("scalar", f_g)
                g_mk = kb.mark("eS")

                def f_cmb(E, m=m, h=h, g_mk=g_mk, i=i):
                    kb.W(E, "vector", g_mk)
                    kb.W(E, "vector", kb.rX.mark(xrl_dma[i]))
                    if i - 2 in out_mks:
                        kb.W(E, "vector", out_mks[i - 2])
                    E.tensor_tensor(out=t1s, in0=s0_ch(m, h), in1=gst[h], op=ALU.mult)
                    kb.fin(E.tensor_tensor(out=onw[i % 2], in0=xrl[i % 2], in1=t1s,
                                           op=ALU.add), "eV")
                body("vector", f_cmb)
                cmb_mks[i] = kb.mark("eV")

                def f_out(E, m=m, h=h, i=i):
                    kb.W(E, "gpsimd", cmb_mks[i])
                    kb.dma(E, "gpsimd", "qG",
                           x1buf[:, m * TOK + h * NT:m * TOK + (h + 1) * NT], onw[i % 2])
                    if i + 2 < 2 * DC:
                        m2, h2 = divmod(i + 2, 2)
                        xrl_dma[i + 2] = kb.rX.issue(
                            E, xrl[i % 2],
                            d["xin"][:, m2 * TOK + h2 * NT:m2 * TOK + (h2 + 1) * NT])
                body("gpsimd", f_out)
                out_mks[i] = kb.mark("qG")
                return g_mk

            stream_panels([d["w3p"][m] for m in range(DC)], H0C, 4,
                          rhs_fn=lambda k, h: h0_ch(k, h),
                          evac_fn=w3_evac, quant_col=2, banks=(0, 1, 2, 3),
                          cast_pre=(h0_mks[(H0C - 1, 1)],))
            if debug:
                def f_dbg3(E):
                    h0last = max(h0_mks.values())[1] if False else None
                    kb.W(E, "gpsimd", kb.mark("eV"))
                    kb.dma(E, "gpsimd", "qG", dbg["dbg_s0"], S0B)
                    kb.dma(E, "gpsimd", "qG", dbg["dbg_h0"], H0B)
                body("gpsimd", f_dbg3)
            l0_out_mk = kb.mark("qG")

            if not with_l1:
                def f_fin(E):
                    kb.W(E, "gpsimd", l0_out_mk)
                body("gpsimd", f_fin)

        if with_l1:
            l1_start_mk = l0_out_mk if with_l0 else None

            def f_q0(E):
                for nm, dst in (("bkt", bk_sb), ("bvt", bv_sb), ("brt", br_sb),
                                ("bot", bo_sb), ("tmkt", tmk_sb), ("tmvt", tmv_sb),
                                ("tmrt", tmr_sb), ("ut", u_sb), ("wt", w_sb),
                                ("sugt", sug_sb), ("wsdpt", wsdp_sb)):
                    kb.dma(E, "scalar", "qA", dst, d[nm])
                kb.dma(E, "scalar", "qA", sdp_sb[:], d["sdpt"])
                kb.dma(E, "scalar", "qA", su2_sb[:], d["sut"])
                kb.dma(E, "scalar", "qA", sg_sb[:], d["sgt"])
                kb.dma(E, "scalar", "qA", h128_sb[:], d["h128"])
                if not with_l0:
                    kb.dma(E, "scalar", "qA", ctl_sb[:], d["ctl"])
            body("scalar", f_q0)
            q0_mk = kb.mark("qA")
            if not with_l0:
                ctl_mk = q0_mk

            xlast_b = sm(8).bitcast(BF16)
            xrecv_b = sm(8).bitcast(BF16)

            def f_q0v(E):
                kb.W(E, "vector", q0_mk)
                if not with_l0:
                    E.memset(ones_b[:], 1.0)
                    E.memset(onesf_col[:], 1.0)
                    E.memset(onesf_row[:], 1.0)
                    E.memset(flgsrc[:], 0)
                E.memset(xrecv_b, 0.0)
                E.memset(a_in, 0.0)
                E.memset(b_in, 0.0)
                E.tensor_scalar(out=nw_sb, in0=w_sb, scalar1=-1.0, scalar2=None,
                                op0=ALU.mult)
                E.tensor_tensor(out=uw_sb, in0=u_sb, in1=w_sb, op=ALU.add)
                kb.fin(E.tensor_tensor(out=su2_sb[:], in0=su2_sb[:], in1=sdp_sb[:],
                                       op=ALU.mult), "eV")
            body("vector", f_q0v)
            q0v_mk = kb.mark("eV")

            X1f = AFP(32832, DC * TOK)

            def x1ch(c, h=None):
                if h is None:
                    return X1f[:, c * TOK:(c + 1) * TOK]
                return X1f[:, c * TOK + h * NT:c * TOK + (h + 1) * NT]

            def f_x1l(E):
                if l1_start_mk is not None:
                    kb.W(E, "scalar", l1_start_mk)
                kb.dma(E, "scalar", "qA", X1f, x1buf)
            body("scalar", f_x1l)
            x1_mk = kb.mark("qA")

            xn1_done = {}
            for h in range(2):
                eS0 = kb.cnt["eS"]; eV0 = kb.cnt["eV"]; eT0 = kb.cnt["eT"]

                def f_sq(E, h=h, eT0=eT0):
                    kb.W(E, "scalar", x1_mk)
                    for c in range(DC):
                        if c >= 2:
                            E.wait_ge(sems["eT"], eT0 + c - 1)
                        E.square(out=sq_strips[c % 2], in_=x1ch(c, h)).then_inc(sems["eS"], 1)
                body("scalar", f_sq)
                kb.bump("eS", DC)

                def f_cp(E, h=h, eT0=eT0):
                    kb.W(E, "vector", x1_mk)
                    kb.W(E, "vector", q0v_mk)
                    for c in range(DC):
                        if c >= 2:
                            E.wait_ge(sems["eT"], eT0 + c - 1)
                        E.tensor_copy(out=xb_strips[c % 2], in_=x1ch(c, h)).then_inc(sems["eV"], 1)
                body("vector", f_cp)
                kb.bump("eV", DC)

                bA, bB = (0, 1) if h == 0 else (2, 3)

                def f_st(E, h=h, bA=bA, bB=bB, eS0=eS0, eV0=eV0):
                    bankA = kb.bank(E, "tensor", bA)
                    bankB = kb.bank(E, "tensor", bB)
                    for c in range(DC):
                        E.wait_ge(sems["eS"], eS0 + c + 1)
                        E.wait_ge(sems["eV"], eV0 + c + 1)
                        E.matmul(bankA[0:1, :], ones_b[:], xb_strips[c % 2],
                                 start=(c == 0), stop=(c == DC - 1))
                        E.matmul(bankB[0:1, :], ones_b[:], sq_strips[c % 2],
                                 start=(c == 0), stop=(c == DC - 1)).then_inc(sems["eT"], 1)
                        kb.bump("eT", 1)
                body("tensor", f_st)
                smm_mk = kb.mark("eT")

                def f_sf(E, bA=bA, bB=bB, smm_mk=smm_mk):
                    kb.W(E, "vector", smm_mk)
                    E.tensor_scalar(out=mrow, in0=kb.pb[bA][0:1, :], scalar1=1.0 / D,
                                    scalar2=None, op0=ALU.mult)
                    E.tensor_scalar(out=ssum, in0=kb.pb[bB][0:1, :], scalar1=1.0 / D,
                                    scalar2=None, op0=ALU.mult)
                    E.tensor_tensor(out=sqsum, in0=mrow, in1=mrow, op=ALU.mult)
                    E.tensor_tensor(out=ssum, in0=ssum, in1=sqsum, op=ALU.subtract)
                    kb.fin(E.tensor_scalar(out=ssum, in0=ssum, scalar1=LN_EPS,
                                           scalar2=None, op0=ALU.add), "eV")
                body("vector", f_sf)
                var_mk = kb.mark("eV")
                kb.bank_free[bA] = var_mk
                kb.bank_free[bB] = var_mk

                def f_sqr(E, var_mk=var_mk):
                    kb.W(E, "scalar", var_mk)
                    kb.fin(E.sqrt(out=sqsum, in_=ssum), "eS")
                body("scalar", f_sqr)
                sd_mk = kb.mark("eS")

                def f_rc(E, sd_mk=sd_mk):
                    kb.W(E, "vector", sd_mk)
                    kb.fin(E.reciprocal(out=rrow, in_=sqsum), "eV")
                body("vector", f_rc)
                rstd_mk = kb.mark("eV")

                bM, bR = (4, 5) if h == 0 else (6, 7)

                def f_bc(E, bM=bM, bR=bR, rstd_mk=rstd_mk):
                    kb.W(E, "tensor", rstd_mk)
                    bankM = kb.bank(E, "tensor", bM)
                    bankR = kb.bank(E, "tensor", bR)
                    E.matmul(bankM, onesf_row[:], mrow, start=True, stop=True)
                    kb.fin(E.matmul(bankR, onesf_row[:], rrow, start=True, stop=True), "eT")
                body("tensor", f_bc)
                bc_mk = kb.mark("eT")

                def f_xn(E, h=h, bM=bM, bR=bR, bc_mk=bc_mk):
                    kb.W(E, "vector", bc_mk)
                    for c in range(DC):
                        E.tensor_tensor(out=lntmp, in0=x1ch(c, h), in1=kb.pb[bM],
                                        op=ALU.subtract)
                        E.tensor_tensor(out=XN(c, h * NT, NT), in0=lntmp, in1=kb.pb[bR],
                                        op=ALU.mult)
                    kb.fin(E.engine_nop(), "eV")
                body("vector", f_xn)
                xn1_done[h] = kb.mark("eV")
                kb.bank_free[bM] = xn1_done[h]
                kb.bank_free[bR] = xn1_done[h]

            xn1_mk = xn1_done[1]

            def f_gather(E):
                kb.W(E, "vector", xn1_mk)
                xl_view = arena_b.tensor.ap().bitcast(BF16) if False else None
                src = arena_b[:, TOK:TOK + 1]
                lastv = arena_b[:, 0:DC * XNSTR].rearrange(
                    "p (c s) -> p c s", c=DC)[:, :, TOK:TOK + 1]
                col0v = arena_b[:, 0:DC * XNSTR].rearrange(
                    "p (c s) -> p c s", c=DC)[:, :, 0:1]
                E.tensor_copy(out=xlast_b, in_=lastv)
                E.memset(col0v, 0.0)
                E.tensor_copy(out=lntmp[:, 0:256], in_=lntmp[:, 256:512])
                kb.fin(E.engine_nop(), "eV")
            body("vector", f_gather)
            gather_mk = kb.mark("eV")

            xd_xn = emit_exchange("xn", 4, [(shr["pay_xn"], xlast_b)],
                                  [(xrecv_b, shr["pay_xn"])], ready_mk=gather_mk,
                                  extra_mks=(ctl_mk,))

            def f_col0(E):
                kb.W(E, "vector", xd_xn)
                col0v = arena_b[:, 0:DC * XNSTR].rearrange(
                    "p (c s) -> p c s", c=DC)[:, :, 0:1]
                E.tensor_copy(out=col0v, in_=xrecv_b)
                E.tensor_copy(out=lntmp[:, 0:256], in_=lntmp[:, 256:512])
                kb.fin(E.engine_nop(), "eV")
            body("vector", f_col0)
            col0_mk = kb.mark("eV")

            if debug:
                def f_d1(E):
                    kb.W(E, "gpsimd", col0_mk)
                    kb.dma(E, "gpsimd", "qG", dbg["dbg1_xn"], arena_b[:, 0:DC * XNSTR])
                body("gpsimd", f_d1)

            MIXOFF = 32832
            MIXB = ABF(MIXOFF, DC * TOK)
            KOFF, VOFF, ROFF = 65600, 98368, 131136
            KB_ = ABF(KOFF, DC * TOK)
            VB_ = ABF(VOFF, DC * TOK)
            RB_ = ABF(ROFF, DC * TOK)

            def mix_ch(c, h=None):
                if h is None:
                    return MIXB[:, c * TOK:(c + 1) * TOK]
                return MIXB[:, c * TOK + h * NT:c * TOK + (h + 1) * NT]

            def kch(c, h=None):
                if h is None:
                    return KB_[:, c * TOK:(c + 1) * TOK]
                return KB_[:, c * TOK + h * NT:c * TOK + (h + 1) * NT]

            def vch(c, h=None):
                if h is None:
                    return VB_[:, c * TOK:(c + 1) * TOK]
                return VB_[:, c * TOK + h * NT:c * TOK + (h + 1) * NT]

            def rch(c, h=None):
                if h is None:
                    return RB_[:, c * TOK:(c + 1) * TOK]
                return RB_[:, c * TOK + h * NT:c * TOK + (h + 1) * NT]

            dstrip = AFP(170048, TOK)
            rawstrip = AFP(174144, TOK)

            def emit_mix(tm_sb, presilu, prev_mm_mk):
                def f_mx(E):
                    kb.W(E, "vector", col0_mk)
                    if prev_mm_mk is not None:
                        kb.W(E, "vector", prev_mm_mk)
                    for c in range(DC):
                        xs = XN(c, -1, TOK)
                        xn_ = XN(c, 0, TOK)
                        E.tensor_tensor(out=dstrip, in0=xn_, in1=xs, op=ALU.subtract)
                        if presilu:
                            E.scalar_tensor_tensor(out=rawstrip, in0=dstrip,
                                                   scalar=tm_sb[:, c:c + 1], in1=xs,
                                                   op0=ALU.mult, op1=ALU.add
                                                   ).then_inc(sems["eV"], 1)
                            kb.bump("eV", 1)
                        else:
                            E.scalar_tensor_tensor(out=mix_ch(c), in0=dstrip,
                                                   scalar=tm_sb[:, c:c + 1], in1=xs,
                                                   op0=ALU.mult, op1=ALU.add
                                                   ).then_inc(sems["eV"], 1)
                            kb.bump("eV", 1)
                        if presilu:
                            mkv = kb.mark("eV")

                            def f_si(E2, c=c, mkv=mkv):
                                kb.W(E2, "scalar", mkv)
                                kb.fin(E2.activation(out=mix_ch(c), in_=rawstrip,
                                                     func=AF.Silu), "eS")
                            body("scalar", f_si)
                    kb.fin(E.engine_nop(), "eV")
                if not presilu:
                    body("vector", f_mx)
                    return kb.mark("eV")
                else:
                    raise RuntimeError("use emit_mix_silu")

            def emit_mix_silu(tm_sb, prev_mm_mk):
                raw_mks = {}

                def f_mx(E):
                    kb.W(E, "vector", col0_mk)
                    if prev_mm_mk is not None:
                        kb.W(E, "vector", prev_mm_mk)
                    for c in range(DC):
                        if c >= 1:
                            kb.W(E, "vector", raw_mks[c - 1])
                        xs = XN(c, -1, TOK)
                        xn_ = XN(c, 0, TOK)
                        E.tensor_tensor(out=dstrip, in0=xn_, in1=xs, op=ALU.subtract)
                        E.scalar_tensor_tensor(out=rawstrip, in0=dstrip,
                                               scalar=tm_sb[:, c:c + 1], in1=xs,
                                               op0=ALU.mult, op1=ALU.add
                                               ).then_inc(sems["eV"], 1)
                        kb.bump("eV", 1)
                        mkv = kb.mark("eV")

                        def f_si(E2, c=c, mkv=mkv):
                            kb.W(E2, "scalar", mkv)
                            kb.fin(E2.activation(out=mix_ch(c), in_=rawstrip,
                                                 func=AF.Silu), "eS")
                        raw_mks[c] = ("eS", kb.cnt["eS"] + 1)
                        body_later.append(("scalar", f_si))
                    kb.fin(E.engine_nop(), "eV")
                raise RuntimeError("unused")

            def emit_mix2(tm_sb, presilu, prev_mm_mk):
                raws = [AFP(174144, TOK), AFP(178240, TOK)]
                mks = {}
                for c in range(DC):
                    def f_v(E, c=c):
                        if c == 0:
                            kb.W(E, "vector", col0_mk)
                            if prev_mm_mk is not None:
                                kb.W(E, "vector", prev_mm_mk)
                        if presilu and c >= 2:
                            kb.W(E, "vector", mks[c - 2])
                        xs = XN(c, -1, TOK)
                        xn_ = XN(c, 0, TOK)
                        E.tensor_tensor(out=dstrip, in0=xn_, in1=xs, op=ALU.subtract)
                        tgt = raws[c % 2] if presilu else mix_ch(c)
                        kb.fin(E.scalar_tensor_tensor(out=tgt, in0=dstrip,
                                                      scalar=tm_sb[:, c:c + 1], in1=xs,
                                                      op0=ALU.mult, op1=ALU.add), "eV")
                    body("vector", f_v)
                    vmk = kb.mark("eV")
                    if presilu:
                        def f_s(E, c=c, vmk=vmk):
                            kb.W(E, "scalar", vmk)
                            kb.fin(E.activation(out=mix_ch(c), in_=raws[c % 2],
                                                func=AF.Silu), "eS")
                        body("scalar", f_s)
                        mks[c] = kb.mark("eS")
                    else:
                        mks[c] = vmk
                return mks[DC - 1]

            def proj_evac(dstf, bias_sb):
                mks = {}

                def evac(m, h, bidx, mm_mk):
                    def f(E, m=m, h=h, bidx=bidx, mm_mk=mm_mk):
                        kb.W(E, "vector", mm_mk)
                        kb.fin(E.tensor_scalar(out=dstf(m, h), in0=kb.pb[bidx],
                                               scalar1=bias_sb[:, m:m + 1],
                                               scalar2=None, op0=ALU.add), "eV")
                    body("vector", f)
                    mks[(m, h)] = kb.mark("eV")
                    return mks[(m, h)]
                return evac, mks

            kmix_mk = emit_mix2(tmk_sb, False, None)
            k_evac, k_mks = proj_evac(kch, bk_sb)
            stream_panels([d["wkp"][m] for m in range(DC)], DC, 4,
                          rhs_fn=lambda k, h: mix_ch(k, h), evac_fn=k_evac,
                          cast_pre=(kmix_mk,), banks=(0, 1, 2, 3))
            wk_mm_mk = kb.stream_guard[1]
            vmix_mk = emit_mix2(tmv_sb, False, wk_mm_mk)
            v_evac, v_mks = proj_evac(vch, bv_sb)
            stream_panels([d["wvp"][m] for m in range(DC)], DC, 4,
                          rhs_fn=lambda k, h: mix_ch(k, h), evac_fn=v_evac,
                          cast_pre=(vmix_mk,), banks=(0, 1, 2, 3))
            wv_mm_mk = kb.stream_guard[1]
            rmix_mk = emit_mix2(tmr_sb, True, wv_mm_mk)
            r_evac, r_mks = proj_evac(rch, br_sb)
            stream_panels([d["wrp"][m] for m in range(DC)], DC, 4,
                          rhs_fn=lambda k, h: mix_ch(k, h), evac_fn=r_evac,
                          cast_pre=(rmix_mk,), banks=(0, 1, 2, 3))
            wr_mm_mk = kb.stream_guard[1]
            dbg_q3_mk = None
            if debug:
                def f_d2(E):
                    kb.W(E, "gpsimd", k_mks[(DC - 1, 1)])
                    kb.W(E, "gpsimd", v_mks[(DC - 1, 1)])
                    kb.W(E, "gpsimd", r_mks[(DC - 1, 1)])
                    kb.dma(E, "gpsimd", "qG", dbg["dbg1_k"], KB_)
                    kb.dma(E, "gpsimd", "qG", dbg["dbg1_v"], VB_)
                    kb.dma(E, "gpsimd", "qG", dbg["dbg1_r"], RB_)
                body("gpsimd", f_d2)
                dbg_q3_mk = kb.mark("qG")

            tts = [AFP(32832, TOK), AFP(36928, TOK)]
            e1s = [AFP(41024, TOK), AFP(45120, TOK)]
            ex2s = [AFP(49216, TOK), AFP(53312, TOK)]
            sa = AFP(57408, TOK)
            sb_ = AFP(61504, TOK)

            def emit_scan_pass(pass2):
                va_mks = {}
                s_mks = {}

                def VA(E, c):
                    st = c % 2
                    if c == 0:
                        kb.W(E, "vector", wr_mm_mk)
                    if pass2 and c == 0:
                        kb.W(E, "vector", xd_ab)
                        if dbg_q3_mk is not None:
                            kb.W(E, "vector", dbg_q3_mk)
                    kb.W(E, "vector", k_mks[(c, 1)])
                    kb.fin(E.tensor_scalar(out=tts[st], in0=kch(c),
                                           scalar1=u_sb[:, c:c + 1],
                                           scalar2=w_sb[:, c:c + 1],
                                           op0=ALU.add, op1=ALU.max), "eV")

                def S(E, c):
                    st = c % 2
                    kb.W(E, "scalar", va_mks[c])
                    E.activation(out=e1s[st], in_=tts[st], func=AF.Exp,
                                 scale=-1.0, bias=nw_sb[:, c:c + 1])
                    kb.fin(E.activation(out=ex2s[st], in_=kch(c), func=AF.Exp,
                                        bias=uw_sb[:, c:c + 1]), "eS")

                def VC(E, c):
                    st = c % 2
                    kb.W(E, "vector", s_mks[c])
                    kb.W(E, "vector", v_mks[(c, 1)])
                    E.tensor_tensor(out=ex2s[st], in0=e1s[st], in1=ex2s[st],
                                    op=ALU.mult)
                    E.tensor_tensor(out=tts[st], in0=ex2s[st], in1=vch(c),
                                    op=ALU.mult)
                    ia = a_in[:, c:c + 1] if pass2 else 0.0
                    ib = b_in[:, c:c + 1] if pass2 else 0.0
                    E.tensor_tensor_scan(out=sa, data0=e1s[st], data1=tts[st],
                                         initial=ia, op0=ALU.mult, op1=ALU.add)
                    E.tensor_tensor_scan(out=sb_, data0=e1s[st], data1=ex2s[st],
                                         initial=ib, op0=ALU.mult, op1=ALU.add)
                    if not pass2:
                        E.tensor_copy(out=afin[:, c:c + 1], in_=sa[:, TOK - 1:TOK])
                        E.tensor_copy(out=lntmp[:, 0:256], in_=lntmp[:, 256:512])
                        E.tensor_copy(out=bfin[:, c:c + 1], in_=sb_[:, TOK - 1:TOK])
                        kb.fin(E.tensor_copy(out=lntmp[:, 256:512],
                                             in_=lntmp[:, 0:256]), "eV")
                    else:
                        kb.W(E, "vector", r_mks[(c, 1)])
                        E.reciprocal(out=ex2s[st], in_=sb_)
                        E.tensor_tensor(out=sa, in0=sa, in1=ex2s[st], op=ALU.mult)
                        kb.fin(E.tensor_tensor(out=rch(c), in0=sa, in1=rch(c),
                                               op=ALU.mult), "eV")

                for c in range(DC):
                    def f_va(E, c=c):
                        VA(E, c)
                    body("vector", f_va)
                    va_mks[c] = kb.mark("eV")

                    def f_s(E, c=c):
                        S(E, c)
                    body("scalar", f_s)
                    s_mks[c] = kb.mark("eS")
                    if c >= 1:
                        def f_vc(E, c=c):
                            VC(E, c - 1)
                        body("vector", f_vc)
                def f_vcl(E):
                    VC(E, DC - 1)
                body("vector", f_vcl)
                return kb.mark("eV")

            p1ab_mk = emit_scan_pass(False)
            xd_ab = emit_exchange("ab", 5,
                                  [(shr["pay_ab"][:, 0:DC], afin),
                                   (shr["pay_ab"][:, DC:2 * DC], bfin)],
                                  [(a_in, shr["pay_ab"][:, 0:DC]),
                                   (b_in, shr["pay_ab"][:, DC:2 * DC])],
                                  ready_mk=p1ab_mk, extra_mks=(ctl_mk,))
            wkvr_mk = emit_scan_pass(True)
            dbg_q4_mk = None
            if debug:
                def f_d3(E):
                    kb.W(E, "gpsimd", wkvr_mk)
                    kb.dma(E, "gpsimd", "qG", dbg["dbg1_ab"][:, 0:DC], afin)
                    kb.dma(E, "gpsimd", "qG", dbg["dbg1_ab"][:, DC:2 * DC], bfin)
                    kb.dma(E, "gpsimd", "qG", dbg["dbg1_ab"][:, 2 * DC:3 * DC], a_in)
                    kb.dma(E, "gpsimd", "qG", dbg["dbg1_ab"][:, 3 * DC:4 * DC], b_in)
                    kb.dma(E, "gpsimd", "qG", dbg["dbg1_wkvr"], RB_)
                body("gpsimd", f_d3)
                dbg_q4_mk = kb.mark("qG")

            s1st = [ABF(170048, NT), ABF(170048 + 1024, NT)]
            s1_out_mks = {}

            def wo_evac(m, h, bidx, mm_mk):
                i = 2 * m + h

                def f(E, m=m, h=h, bidx=bidx, mm_mk=mm_mk, i=i):
                    kb.W(E, "vector", mm_mk)
                    kb.W(E, "vector", wkvr_mk)
                    if i - 2 in s1_out_mks:
                        kb.W(E, "vector", s1_out_mks[i - 2])
                    kb.fin(E.tensor_scalar(out=s1st[i % 2], in0=kb.pb[bidx],
                                           scalar1=bo_sb[:, m:m + 1], scalar2=None,
                                           op0=ALU.add), "eV")
                body("vector", f)
                ev = kb.mark("eV")

                def f_o(E, m=m, h=h, ev=ev, i=i):
                    kb.W(E, "gpsimd", ev)
                    kb.dma(E, "gpsimd", "qG",
                           s1sp[:, m * TOK + h * NT:m * TOK + (h + 1) * NT], s1st[i % 2])
                body("gpsimd", f_o)
                s1_out_mks[i] = kb.mark("qG")
                return ev

            stream_panels([d["wop"][m] for m in range(DC)], DC, 4,
                          rhs_fn=lambda k, h: rch(k, h), evac_fn=wo_evac,
                          cast_pre=(wkvr_mk,), banks=(0, 1, 2, 3))
            wo_done_mk = kb.mark("qG")

            ZOFF = 163904
            ZB = ABF(ZOFF, DC * TOK)

            def zch(c, h=None):
                if h is None:
                    return ZB[:, c * TOK:(c + 1) * TOK]
                return ZB[:, c * TOK + h * NT:c * TOK + (h + 1) * NT]

            FA, FB = 65600, 98368
            z_done_mks = {}
            zprev = {}
            for st_i in range(2):
                sA = AFP(FA, DC * NT)
                sB = AFP(FB, DC * NT)

                def f_su(E, st_i=st_i, sA=sA):
                    kb.W(E, "vector", wo_done_mk)
                    kb.W(E, "vector", xn1_mk)
                    if "cst" in zprev:
                        kb.W(E, "vector", zprev["cst"])
                    if "evacs" in zprev:
                        kb.W(E, "vector", zprev["evacs"])
                    if "dump" in zprev:
                        kb.W(E, "vector", zprev["dump"])
                    for c in range(DC):
                        E.scalar_tensor_tensor(out=sA[:, c * NT:(c + 1) * NT],
                                               in0=XN(c, st_i * NT, NT),
                                               scalar=sug_sb[:, c:c + 1],
                                               in1=XN(c, st_i * NT, NT),
                                               op0=ALU.mult, op1=ALU.bypass) \
                            if False else \
                            E.tensor_scalar(out=sA[:, c * NT:(c + 1) * NT],
                                            in0=XN(c, st_i * NT, NT),
                                            scalar1=sug_sb[:, c:c + 1], scalar2=None,
                                            op0=ALU.mult)
                    kb.fin(E.engine_nop(), "eV")
                body("vector", f_su)

                bufs = [sA, sB]
                for stg in range(4):
                    src, dst = bufs[stg % 2], bufs[(stg + 1) % 2]
                    stride = 1 << stg

                    def f_bf(E, src=src, dst=dst, stride=stride):
                        g = DC // (2 * stride)
                        s4 = src.rearrange("p (g two r s) -> p g two r s",
                                           g=g, two=2, r=stride)
                        d4 = dst.rearrange("p (g two r s) -> p g two r s",
                                           g=g, two=2, r=stride)
                        E.tensor_tensor(out=d4[:, :, 0], in0=s4[:, :, 0],
                                        in1=s4[:, :, 1], op=ALU.add)
                        E.tensor_tensor(out=d4[:, :, 1], in0=s4[:, :, 0],
                                        in1=s4[:, :, 1], op=ALU.subtract)
                        kb.fin(E.engine_nop(), "eV")
                    body("vector", f_bf)
                bfly_mk = kb.mark("eV")
                sAb = ABF(FA, DC * NT)

                def f_cst(E, sA=sA, sAb=sAb, bfly_mk=bfly_mk):
                    kb.W(E, "scalar", bfly_mk)
                    kb.fin(E.copy(out=sAb, in_=sA), "eS")
                sBb = ABF(FB, DC * NT)

                def f_cst2(E, sA=sA, sBb=sBb, bfly_mk=bfly_mk):
                    kb.W(E, "scalar", bfly_mk)
                    if "mm_last" in zprev:
                        kb.W(E, "scalar", zprev["mm_last"])
                    kb.fin(E.copy(out=sBb, in_=sA), "eS")
                body("scalar", f_cst2)
                cst_mk = kb.mark("eS")
                zprev["cst"] = cst_mk

                for c in range(DC):
                    bidx = c % 4

                    def f_mm(E, c=c, bidx=bidx, cst_mk=cst_mk, sBb=sBb):
                        kb.W(E, "tensor", cst_mk)
                        kb.W(E, "tensor", q0_mk)
                        bank = kb.bank(E, "tensor", bidx)
                        kb.fin(E.matmul(bank, h128_sb[:],
                                        sBb[:, c * NT:(c + 1) * NT],
                                        start=True, stop=True), "eT")
                    body("tensor", f_mm)
                    mm_mk = kb.mark("eT")

                    def f_ev(E, c=c, st_i=st_i, bidx=bidx, mm_mk=mm_mk):
                        kb.W(E, "scalar", mm_mk)
                        kb.fin(E.activation(out=zch(c, st_i), in_=kb.pb[bidx],
                                            func=AF.Copy,
                                            scale=1.0 / float(np.sqrt(D))), "eS")
                    body("scalar", f_ev)
                    z_done_mks[(c, st_i)] = kb.mark("eS")
                    kb.bank_free[bidx] = z_done_mks[(c, st_i)]
                zprev["mm_last"] = kb.mark("eT")
                zprev["evacs"] = kb.mark("eS")
                if debug and st_i == 0:
                    def f_dsb(E, sBb=sBb):
                        kb.W(E, "gpsimd", kb.mark("eS"))
                        kb.dma(E, "gpsimd", "qG", dbg["dbg1_sb"], sBb)
                    body("gpsimd", f_dsb)
                    zprev["dump"] = kb.mark("qG")
            z_mk = kb.mark("eS")
            if debug:
                def f_dz2(E):
                    kb.W(E, "gpsimd", z_mk)
                    kb.dma(E, "gpsimd", "qG", dbg["dbg1_z2"], ZB)
                body("gpsimd", f_dz2)

            HOFF = 32832
            HB1 = ABF(HOFF, H1C * TOK)

            def hch(m, h=None):
                if h is None:
                    return HB1[:, m * TOK:(m + 1) * TOK]
                return HB1[:, m * TOK + h * NT:m * TOK + (h + 1) * NT]

            gsst = [ABF(24576, NT), ABF(24576 + 1024, NT),
                    ABF(24576 + 2048, NT), ABF(24576 + 3072, NT)]
            ug_jobs = []
            for m in range(H1C):
                ug_jobs.append(d["wgp"][m])
                ug_jobs.append(d["wup"][m])
            gs_mks = {}
            h1_mks = {}

            def ug_evac(j, h, bidx, mm_mk):
                m, which = j // 2, j % 2
                if which == 0:
                    def f(E, m=m, h=h, bidx=bidx, mm_mk=mm_mk):
                        kb.W(E, "scalar", mm_mk)
                        kb.W(E, "scalar", z_mk)
                        if (m - 2, h) in h1_mks:
                            kb.W(E, "scalar", h1_mks[(m - 2, h)])
                        kb.fin(E.activation(out=gsst[2 * (m % 2) + h], in_=kb.pb[bidx],
                                            func=AF.Silu,
                                            scale=sg_sb[:, m:m + 1]), "eS")
                    body("scalar", f)
                    gs_mks[(m, h)] = kb.mark("eS")
                    return gs_mks[(m, h)]
                else:
                    def f(E, m=m, h=h, bidx=bidx, mm_mk=mm_mk):
                        kb.W(E, "vector", mm_mk)
                        kb.W(E, "vector", gs_mks[(m, h)])
                        if dbg_q4_mk is not None:
                            kb.W(E, "vector", dbg_q4_mk)
                        kb.fin(E.scalar_tensor_tensor(out=hch(m, h), in0=kb.pb[bidx],
                                                      scalar=su2_sb[:, m:m + 1],
                                                      in1=gsst[2 * (m % 2) + h],
                                                      op0=ALU.mult, op1=ALU.mult), "eV")
                    body("vector", f)
                    h1_mks[(m, h)] = kb.mark("eV")
                    return h1_mks[(m, h)]

            ug_dma_pre = [z_mk]
            if debug:
                ug_dma_pre += [dbg_q3_mk, ("qG", kb.cnt["qG"])]
            stream_panels(ug_jobs, DC, 1, rhs_fn=lambda k, h: zch(k, h),
                          evac_fn=ug_evac, cast_pre=(z_mk,), dma_pre=tuple(ug_dma_pre),
                          str_off=0, str_cap=24576)
            ug_mm_mk = kb.stream_guard[1]
            dbg_q7_mk = None
            if debug:
                def f_d4(E):
                    kb.W(E, "gpsimd", z_mk)
                    kb.W(E, "gpsimd", h1_mks[(H1C - 1, 1)])
                    kb.W(E, "gpsimd", wo_done_mk)
                    kb.dma(E, "gpsimd", "qG", dbg["dbg1_z"], ZB)
                    kb.dma(E, "gpsimd", "qG", dbg["dbg1_h"], HB1)
                    kb.dma(E, "gpsimd", "qG", dbg["dbg1_s1"], s1sp)
                body("gpsimd", f_d4)
                dbg_q7_mk = kb.mark("qG")

            TS = 128
            NSTRIP = TOK // TS
            BFA, BFB = 163904, 180288
            bf_last = None
            for si in range(NSTRIP):
                sA = ABF(BFA, H1C * TS)
                sB = ABF(BFB, H1C * TS)
                hsv = HB1.rearrange("p (c t) -> p c t", c=H1C)[:, :, si * TS:(si + 1) * TS]

                def f_in(E, si=si, sA=sA, hsv=hsv):
                    if si == 0:
                        kb.W(E, "vector", ug_mm_mk)
                        kb.W(E, "vector", wo_done_mk)
                        if dbg_q7_mk is not None:
                            kb.W(E, "vector", dbg_q7_mk)
                    g = H1C // 2
                    s4 = hsv.rearrange("p (g two r) t -> p g two r t", g=g, two=2, r=1) \
                        if False else hsv.rearrange("p (g two r) t -> p g two r t",
                                                    g=H1C // 2, two=2, r=1)
                    a4 = sA.rearrange("p (g two r s) -> p g two r s",
                                      g=H1C // 2, two=2, r=1)
                    E.tensor_tensor(out=a4[:, :, 0], in0=s4[:, :, 0], in1=s4[:, :, 1],
                                    op=ALU.add)
                    E.tensor_tensor(out=a4[:, :, 1], in0=s4[:, :, 0], in1=s4[:, :, 1],
                                    op=ALU.subtract)
                    kb.fin(E.engine_nop(), "eV")
                body("vector", f_in)

                bufs = [sA, sB]
                for stg in range(1, 6):
                    src, dst = bufs[(stg - 1) % 2], bufs[stg % 2]
                    stride = 1 << stg

                    def f_bf(E, src=src, dst=dst, stride=stride):
                        g = H1C // (2 * stride)
                        s4 = src.rearrange("p (g two r s) -> p g two r s",
                                           g=g, two=2, r=stride)
                        d4 = dst.rearrange("p (g two r s) -> p g two r s",
                                           g=g, two=2, r=stride)
                        E.tensor_tensor(out=d4[:, :, 0], in0=s4[:, :, 0],
                                        in1=s4[:, :, 1], op=ALU.add)
                        E.tensor_tensor(out=d4[:, :, 1], in0=s4[:, :, 0],
                                        in1=s4[:, :, 1], op=ALU.subtract)
                        kb.fin(E.engine_nop(), "eV")
                    body("vector", f_bf)

                def f_wb(E, si=si, sB=sB, hsv=hsv):
                    kb.fin(E.tensor_copy(out=hsv, in_=sB.rearrange(
                        "p (c s) -> p c s", c=H1C)), "eV")
                body("vector", f_wb)
                bf_last = kb.mark("eV")

            h128_done = {}
            for c in range(H1C):
                for h in range(2):
                    bidx = (2 * c + h) % 4

                    def f_mm(E, c=c, h=h, bidx=bidx):
                        kb.W(E, "tensor", bf_last)
                        bank = kb.bank(E, "tensor", bidx)
                        kb.fin(E.matmul(bank, h128_sb[:], hch(c, h),
                                        start=True, stop=True), "eT")
                    body("tensor", f_mm)
                    mm_mk = kb.mark("eT")

                    def f_ev(E, c=c, h=h, bidx=bidx, mm_mk=mm_mk):
                        kb.W(E, "scalar", mm_mk)
                        kb.fin(E.activation(out=hch(c, h), in_=kb.pb[bidx],
                                            func=AF.Copy,
                                            scale=1.0 / float(np.sqrt(H1))), "eS")
                    body("scalar", f_ev)
                    h128_done[(c, h)] = kb.mark("eS")
                    kb.bank_free[bidx] = h128_done[(c, h)]
            hh_mk = kb.mark("eS")
            if debug:
                def f_d5(E):
                    kb.W(E, "gpsimd", hh_mk)
                    kb.dma(E, "gpsimd", "qG", dbg["dbg1_hh"], HB1)
                body("gpsimd", f_d5)
                hh_mk = ("qG", kb.mark("qG")[1])

            gdst = [ABF(163904, NT), ABF(163904 + 1024, NT)]
            x1rl = [AFP(166016, NT), AFP(166016 + 2048, NT)]
            s1rl = [ABF(170112, NT), ABF(170112 + 1024, NT)]
            t2s = AFP(172160, TOK // 2)
            onw2 = [AFP(174208, NT), AFP(174208 + 2048, NT)]
            cmb2_mks = {}
            yout_mks = {}
            rl_dma = {}

            def f_seed2(E):
                kb.W(E, "gpsimd", hh_mk)
                kb.W(E, "gpsimd", wo_done_mk)
                for i in range(2):
                    m, h = divmod(i, 2)
                    rl_dma[(i, 0)] = kb.rX.issue(
                        E, x1rl[i % 2],
                        x1buf[:, m * TOK + h * NT:m * TOK + (h + 1) * NT])
                    rl_dma[(i, 1)] = kb.rX.issue(
                        E, s1rl[i % 2],
                        s1sp[:, m * TOK + h * NT:m * TOK + (h + 1) * NT])
            body("gpsimd", f_seed2)

            def wd_evac(m, h, bidx, mm_mk):
                i = 2 * m + h

                def f_g(E, m=m, h=h, bidx=bidx, mm_mk=mm_mk, i=i):
                    kb.W(E, "scalar", mm_mk)
                    if i - 2 in cmb2_mks:
                        kb.W(E, "scalar", cmb2_mks[i - 2])
                    kb.fin(E.activation(out=gdst[h], in_=kb.pb[bidx], func=AF.Silu,
                                        scale=wsdp_sb[:, m:m + 1]), "eS")
                body("scalar", f_g)
                g_mk = kb.mark("eS")

                def f_c(E, m=m, h=h, g_mk=g_mk, i=i):
                    kb.W(E, "vector", g_mk)
                    kb.W(E, "vector", kb.rX.mark(rl_dma[(i, 0)]))
                    kb.W(E, "vector", kb.rX.mark(rl_dma[(i, 1)]))
                    if i - 2 in yout_mks:
                        kb.W(E, "vector", yout_mks[i - 2])
                    E.tensor_tensor(out=t2s, in0=s1rl[i % 2], in1=gdst[h], op=ALU.mult)
                    kb.fin(E.tensor_tensor(out=onw2[i % 2], in0=x1rl[i % 2], in1=t2s,
                                           op=ALU.add), "eV")
                body("vector", f_c)
                cmb2_mks[i] = kb.mark("eV")

                def f_o(E, m=m, h=h, i=i):
                    kb.W(E, "gpsimd", cmb2_mks[i])
                    kb.dma(E, "gpsimd", "qG",
                           yout[:, m * TOK + h * NT:m * TOK + (h + 1) * NT], onw2[i % 2])
                    if i + 2 < 2 * DC:
                        m2, h2 = divmod(i + 2, 2)
                        rl_dma[(i + 2, 0)] = kb.rX.issue(
                            E, x1rl[i % 2],
                            x1buf[:, m2 * TOK + h2 * NT:m2 * TOK + (h2 + 1) * NT])
                        rl_dma[(i + 2, 1)] = kb.rX.issue(
                            E, s1rl[i % 2],
                            s1sp[:, m2 * TOK + h2 * NT:m2 * TOK + (h2 + 1) * NT])
                body("gpsimd", f_o)
                yout_mks[i] = kb.mark("qG")
                return g_mk

            stream_panels([d["wdp"][m] for m in range(DC)], H1C, 4,
                          rhs_fn=lambda k, h: hch(k, h), evac_fn=wd_evac,
                          banks=(0, 1, 2, 3), cast_pre=(hh_mk,), str_off=0, str_cap=24576)
            l1_out_mk = kb.mark("qG")

            def f_fin1(E):
                kb.W(E, "gpsimd", l1_out_mk)
            body("gpsimd", f_fin1)

    return nc



def _mk_ctl(c, nonces):
    b, h = divmod(c, 2)
    ctl = np.zeros((1, 12), np.int32)
    ctl[0, 0] = 1 if h == 0 else 0
    ctl[0, 1] = 1 if h == 1 else 0
    ctl[0, 2:2 + len(nonces)] = nonces
    return ctl


def _prep_l0(inputs, nonces):
    shared = {
        "w0p": pack_panels(_f32(inputs["L0_attn_W"]).T),
        "w1p": pack_panels(_f32(inputs["L0_ffn_w1"]).T),
        "w2p": pack_panels(_f32(inputs["L0_ffn_w2"]).T),
        "w3p": pack_panels(_f32(inputs["L0_ffn_w3"]).T),
        "b0t": pack_vec(inputs["L0_attn_b"]),
    }
    x = _f32(inputs["L0_attn_x"])
    maps = []
    for c in range(8):
        b, h = divmod(c, 2)
        m = dict(shared)
        m["xin"] = pack_x_shard(x[b, h * TOK:(h + 1) * TOK, :])
        m["ctl"] = _mk_ctl(c, nonces)
        maps.append(m)
    return maps


def run_l0(inputs, debug=False):
    nonces = np.random.randint(1, 2 ** 30, size=4).astype(np.int32)
    nc = build_program(with_l0=True, with_l1=False, debug=debug)
    maps = _prep_l0(inputs, nonces)
    res = run_bass_kernel_spmd(nc, maps, core_ids=list(range(8)))
    out = np.empty((B, T, D), np.float32)
    for c in range(8):
        bb, h = divmod(c, 2)
        out[bb, h * TOK:(h + 1) * TOK, :] = unpack_x_shard(res.results[c]["x1out"])
    if debug:
        return out, [dict(r) for r in res.results]
    return out


def _prep_l1(inputs, nonces, x1_packed=None):
    shared = {
        "wkp": pack_panels(_f32(inputs["L1_attn_Wk"]).T),
        "wvp": pack_panels(_f32(inputs["L1_attn_Wv"]).T),
        "wrp": pack_panels(_f32(inputs["L1_attn_Wr"]).T),
        "wop": pack_panels(_f32(inputs["L1_attn_Wo"]).T),
        "wup": pack_panels(_f32(inputs["L1_ffn_W_up"]).T),
        "wgp": pack_panels(_f32(inputs["L1_ffn_W_gate"]).T),
        "wdp": pack_panels(_f32(inputs["L1_ffn_W_down"]).T),
        "bkt": pack_vec(inputs["L1_attn_bk"]),
        "bvt": pack_vec(inputs["L1_attn_bv"]),
        "brt": pack_vec(inputs["L1_attn_br"]),
        "bot": pack_vec(inputs["L1_attn_bo"]),
        "tmkt": pack_vec(inputs["L1_attn_tmk"]),
        "tmvt": pack_vec(inputs["L1_attn_tmv"]),
        "tmrt": pack_vec(inputs["L1_attn_tmr"]),
        "ut": pack_vec(inputs["L1_attn_u"]),
        "wt": pack_vec(inputs["L1_attn_w"]),
        "sugt": pack_vec(inputs["L1_ffn_SU_ug"]),
        "sdpt": pack_vec(inputs["L1_ffn_SU_dp"]),
        "sut": pack_vec(np.asarray(inputs["L1_ffn_Wscale_ug"], np.float32)[:H1]),
        "sgt": pack_vec(np.asarray(inputs["L1_ffn_Wscale_ug"], np.float32)[H1:]),
        "wsdpt": pack_vec(inputs["L1_ffn_Wscale_dp"]),
        "h128": np.ascontiguousarray(hadamard128_bf16()),
    }
    maps = []
    for c in range(8):
        b, h = divmod(c, 2)
        m = dict(shared)
        if x1_packed is not None:
            m["x1in"] = x1_packed[c]
        m["ctl"] = _mk_ctl(c, nonces)
        maps.append(m)
    return maps


def run_l1(inputs, x1, debug=False):
    nonces = np.random.randint(1, 2 ** 30, size=4).astype(np.int32)
    nc = build_program(with_l0=False, with_l1=True, debug=debug)
    x1p = []
    for c in range(8):
        b, h = divmod(c, 2)
        x1p.append(pack_x_shard(_f32(x1)[b, h * TOK:(h + 1) * TOK, :]))
    maps = _prep_l1(inputs, nonces, x1p)
    res = run_bass_kernel_spmd(nc, maps, core_ids=list(range(8)))
    out = np.empty((B, T, D), np.float32)
    for c in range(8):
        bb, h = divmod(c, 2)
        out[bb, h * TOK:(h + 1) * TOK, :] = unpack_x_shard(res.results[c]["yout"])
    if debug:
        return out, [dict(r) for r in res.results]
    return out


FUSED = True


def kernel(**inputs):
    if not FUSED:
        nonces0 = np.random.randint(1, 2 ** 30, size=4).astype(np.int32)
        nc0 = build_program(with_l0=True, with_l1=False)
        maps0 = _prep_l0(inputs, nonces0)
        res0 = run_bass_kernel_spmd(nc0, maps0, core_ids=list(range(8)))
        x1p = [np.ascontiguousarray(np.asarray(res0.results[c]["x1out"]))
               for c in range(8)]
        nonces1 = np.random.randint(1, 2 ** 30, size=4).astype(np.int32)
        nc1 = build_program(with_l0=False, with_l1=True)
        maps1 = _prep_l1(inputs, nonces1, x1p)
        res1 = run_bass_kernel_spmd(nc1, maps1, core_ids=list(range(8)))
        res = res1
    else:
        nonces = np.random.randint(1, 2 ** 30, size=4).astype(np.int32)
        nc = build_program(with_l0=True, with_l1=True)
        maps0 = _prep_l0(inputs, nonces)
        maps1 = _prep_l1(inputs, nonces)
        maps = [dict(m0, **m1) for m0, m1 in zip(maps0, maps1)]
        res = run_bass_kernel_spmd(nc, maps, core_ids=list(range(8)))
    out = np.empty((B, T, D), np.float32)
    for c in range(8):
        bb, h = divmod(c, 2)
        out[bb, h * TOK:(h + 1) * TOK, :] = unpack_x_shard(res.results[c]["yout"])
    return out
